# version 07
# baseline (speedup 1.0000x reference)
import numpy as np
import ml_dtypes
import concourse.bass as bass
import concourse.mybir as mybir
from concourse.bass_utils import run_bass_kernel_spmd
from contextlib import ExitStack

F32 = mybir.dt.float32
BF16 = mybir.dt.bfloat16
AF = mybir.ActivationFunctionType
ALU = mybir.AluOpType
AX = mybir.AxisListType.X

D = 2048
KT = 16
PROJ = 4192
PROJP = 4224
MT_PROJ = 33
NH = 32
PH = 64
NS = 32
KC = 4
NE = 16
IE = 1408
IT_E = 11
SHD = 2816
MT_SH = 22
L = 4096
CHUNK = 512
NQ = 4
Q = 128
N_CORES = 8

LN_EPS = 1e-5
RMS_EPS = 1e-6
NEGBIG = -1e9
MBIG = 1e4

N_CHUNKS_DEFAULT = L // CHUNK


class Prog:
    STREAM = dict(PE="PE", DVE="DVE", ACT="ACT", SY="SY", SC="ACT", GP="GP")
    DSEM = dict(PE="PE", DVE="DVE", ACT="ACT", SY="SYD", SC="SCD", GP="POOL")

    def __init__(self, nc, stack):
        self.nc = nc
        self.eng = dict(PE=nc.tensor, DVE=nc.vector, ACT=nc.scalar,
                        SY=nc.sync, SC=nc.scalar, GP=nc.gpsimd)
        self.streams = dict(PE=nc.tensor, DVE=nc.vector, ACT=nc.scalar,
                            SY=nc.sync, GP=nc.gpsimd)
        names = ["PE", "DVE", "ACT", "POOL"] + [
            "d:" + s for s in ("wp0", "wp1", "wp2", "wp3", "w20", "w21", "w22",
                               "w23", "xld0", "xld1", "ost0", "ost1", "dtb",
                               "c0", "cst", "prol", "g0", "g1", "g2", "g3",
                               "h0", "h1", "h2", "h3", "sw0", "sw1", "sw2",
                               "sw3", "sv0", "sv1", "sv2", "sv3",
                               "c0a", "c0b")]
        self.sem = {k: stack.enter_context(
            nc.semaphore("S" + k.replace(":", "_"))) for k in names}
        self.cnt = dict.fromkeys(self.sem, 0)
        self.wm = {}
        self.buf = {}
        self.phase = "init"
        self.waitlog = []

    def _wait(self, stream, semname, val):
        key = (stream, semname)
        if self.wm.get(key, 0) >= val:
            return
        self.streams[stream].wait_ge(self.sem[semname], val)
        self.waitlog.append((stream, semname, val, self.phase))
        self.wm[key] = val

    def emit(self, engine, fn, reads=(), writes=(), conflicts=(), dma=None):
        stream = self.STREAM[engine]
        need = {}

        def add(tag):
            if tag is None:
                return
            s, c = tag
            if c > need.get(s, 0):
                need[s] = c

        for k in reads:
            ent = self.buf.get(k)
            if ent is not None:
                add(ent[0])
        for k in writes:
            ent = self.buf.get(k)
            if ent is not None:
                add(ent[0])
                for r in ent[1]:
                    add(r)
        if conflicts:
            dead = [k for k in self.buf
                    if (k[0] if isinstance(k, tuple) else k) in conflicts]
            for k in dead:
                ent = self.buf[k]
                add(ent[0])
                for r in ent[1]:
                    add(r)
                del self.buf[k]
        for s in sorted(need):
            self._wait(stream, s, need[s])
        instr = fn(self.eng[engine])
        if dma is not None:
            semname, step = "d:" + dma, 16
        else:
            semname, step = self.DSEM[engine], 1
        self.cnt[semname] += step
        instr.then_inc(self.sem[semname], step)
        tag = (semname, self.cnt[semname])
        for k in writes:
            self.buf[k] = [tag, []]
        for k in reads:
            ent = self.buf.get(k)
            if ent is not None:
                ent[1].append(tag)
        return tag

    def barrier_all(self):
        for stream in self.streams:
            for s, c in self.cnt.items():
                if c:
                    self._wait(stream, s, c)


def _tiles_mkpj(wT):
    K, M = wT.shape
    t = wT.reshape(K // 128, 128, M // 128, 128)
    return np.ascontiguousarray(t.transpose(2, 1, 0, 3).reshape(M // 128, 128, K))


def _prep_inputs(ip):
    f = np.float32
    W_in = np.asarray(ip["L0_attn_W_in"], f)
    W_in_p = np.zeros((D, PROJP), f)
    W_in_p[:, :PROJ] = W_in
    b_in = np.zeros((PROJP,), f)
    b_in[:PROJ] = np.asarray(ip["L0_attn_b_in"], f)
    conv_w = np.asarray(ip["L0_attn_conv_w"], f)
    conv_wp = np.zeros((17 * 128, KC), f)
    conv_wp[:2112] = conv_w
    conv_b = np.zeros((17 * 128,), f)
    conv_b[:2112] = np.asarray(ip["L0_attn_conv_b"], f)
    W1 = np.asarray(ip["L0_ffn_W1"], f)
    W2 = np.asarray(ip["L0_ffn_W2"], f)
    W3 = np.asarray(ip["L0_ffn_W3"], f)

    d = {}
    d["w_in"] = _tiles_mkpj(W_in_p)
    d["w_out"] = _tiles_mkpj(np.asarray(ip["L0_attn_W_out"], f))
    d["w1"] = np.stack([_tiles_mkpj(W1[e].T) for e in range(NE)])
    d["w3"] = np.stack([_tiles_mkpj(W3[e].T) for e in range(NE)])
    d["w2"] = np.stack([_tiles_mkpj(W2[e].T) for e in range(NE)])
    d["ws1"] = _tiles_mkpj(np.asarray(ip["L0_ffn_Ws1"], f).T)
    d["ws3"] = _tiles_mkpj(np.asarray(ip["L0_ffn_Ws3"], f).T)
    d["ws2"] = _tiles_mkpj(np.asarray(ip["L0_ffn_Ws2"], f).T)
    gwT = np.asarray(ip["L0_ffn_gate_w"], f).T
    d["gwt"] = np.ascontiguousarray(
        gwT.reshape(KT, 128, NE).transpose(1, 0, 2).reshape(128, KT * NE))
    d["b_in_c"] = np.ascontiguousarray(b_in.reshape(MT_PROJ, 128).T)
    d["conv_w_c"] = np.ascontiguousarray(
        conv_wp.reshape(17, 128, KC).transpose(1, 0, 2).reshape(128, 17 * KC))
    d["conv_b_c"] = np.ascontiguousarray(conv_b.reshape(17, 128).T)
    d["a_log_c"] = np.asarray(ip["L0_attn_A_log"], f).reshape(NS, 1)
    d["dt_bias_c"] = np.asarray(ip["L0_attn_dt_bias"], f).reshape(NH, 1)
    d["d_rep_c"] = np.ascontiguousarray(
        np.repeat(np.asarray(ip["L0_attn_D_param"], f), PH).reshape(KT, 128).T)
    d["norm_w_c"] = np.ascontiguousarray(
        np.asarray(ip["L0_attn_norm_w"], f).reshape(KT, 128).T)
    d["epsc"] = np.ascontiguousarray(np.broadcast_to(np.array([LN_EPS, RMS_EPS], f), (128, 2)))
    d["id_f"] = np.eye(128, dtype=f)
    d["id_b"] = np.eye(128, dtype=ml_dtypes.bfloat16)
    t2 = np.arange(128).reshape(128, 1)
    t1 = np.arange(128).reshape(1, 128)
    d["negmask"] = np.where(t1 >= t2, 0.0, NEGBIG).astype(f)
    d["negmask_t"] = np.ascontiguousarray(d["negmask"].T)
    rs = np.zeros((NS, NH * NS), ml_dtypes.bfloat16)
    for h in range(NH):
        rs[h, h * NS:(h + 1) * NS] = 1.0
    d["rowsel"] = rs
    oh16 = np.zeros((NE, NE * 128), ml_dtypes.bfloat16)
    for e in range(NE):
        oh16[e, e * 128:(e + 1) * 128] = 1.0
    d["onehot16"] = oh16
    return d


def build_nc(n_chunks=N_CHUNKS_DEFAULT):
    nc = bass.Bass("TRN2", target_bir_lowering=False)
    dt = nc.dram_tensor

    x = dt("x", [L, D], F32, kind="ExternalInput").ap()
    out = dt("out", [L, D], F32, kind="ExternalOutput").ap()
    w_in = dt("w_in", [MT_PROJ, 128, D], F32, kind="ExternalInput").ap()
    w_out = dt("w_out", [KT, 128, D], F32, kind="ExternalInput").ap()
    w1 = dt("w1", [NE, IT_E, 128, D], F32, kind="ExternalInput").ap()
    w3 = dt("w3", [NE, IT_E, 128, D], F32, kind="ExternalInput").ap()
    w2 = dt("w2", [NE, KT, 128, IE], F32, kind="ExternalInput").ap()
    ws1 = dt("ws1", [MT_SH, 128, D], F32, kind="ExternalInput").ap()
    ws3 = dt("ws3", [MT_SH, 128, D], F32, kind="ExternalInput").ap()
    ws2 = dt("ws2", [KT, 128, SHD], F32, kind="ExternalInput").ap()
    gwt = dt("gwt", [128, KT * NE], F32, kind="ExternalInput").ap()
    b_in_c = dt("b_in_c", [128, MT_PROJ], F32, kind="ExternalInput").ap()
    conv_w_c = dt("conv_w_c", [128, 17 * KC], F32, kind="ExternalInput").ap()
    conv_b_c = dt("conv_b_c", [128, 17], F32, kind="ExternalInput").ap()
    a_log_c = dt("a_log_c", [NS, 1], F32, kind="ExternalInput").ap()
    dt_bias_c = dt("dt_bias_c", [NH, 1], F32, kind="ExternalInput").ap()
    d_rep_c = dt("d_rep_c", [128, KT], F32, kind="ExternalInput").ap()
    norm_w_c = dt("norm_w_c", [128, KT], F32, kind="ExternalInput").ap()
    epsc_d = dt("epsc", [128, 2], F32, kind="ExternalInput").ap()
    id_f_d = dt("id_f", [128, 128], F32, kind="ExternalInput").ap()
    id_b_d = dt("id_b", [128, 128], BF16, kind="ExternalInput").ap()
    negmask_d = dt("negmask", [128, 128], F32, kind="ExternalInput").ap()
    negmask_t_d = dt("negmask_t", [128, 128], F32, kind="ExternalInput").ap()
    onehot16_d = dt("onehot16", [NE, NE * 128], BF16, kind="ExternalInput").ap()
    rowsel_d = dt("rowsel", [NS, NH * NS], BF16, kind="ExternalInput").ap()

    s_w_in = dt("s_w_in", [MT_PROJ, 128, D], BF16).ap()
    s_w_out = dt("s_w_out", [KT, 128, D], BF16).ap()
    s_w1 = dt("s_w1", [NE, IT_E, 128, D], BF16).ap()
    s_w3 = dt("s_w3", [NE, IT_E, 128, D], BF16).ap()
    s_w2 = dt("s_w2", [NE, KT, 128, IE], BF16).ap()
    s_ws1 = dt("s_ws1", [MT_SH, 128, D], BF16).ap()
    s_ws3 = dt("s_ws3", [MT_SH, 128, D], BF16).ap()
    s_ws2 = dt("s_ws2", [KT, 128, SHD], BF16).ap()

    with ExitStack() as st:
        def sb(name, shape, dtype):
            return st.enter_context(nc.sbuf_tensor(name, shape, dtype))

        def psb(name, shape, dtype):
            return st.enter_context(nc.psum_tensor(name, shape, dtype))

        res = sb("res", [128, KT * CHUNK], F32)
        xstg = sb("xstg", [128, 2 * D], F32)
        xn = sb("xn", [128, KT * CHUNK], BF16)
        ubuf = sb("ubuf", [128, 24576], BF16)
        SG_O, XS_O, YN_O = 0, KT * CHUNK, 2 * KT * CHUNK
        G_O = [0, IT_E * CHUNK]
        SH_O = 2 * IT_E * CHUNK
        stg = sb("stg", [128, 3 * CHUNK], F32)
        sqb = sb("sqb", [128, 2 * CHUNK], BF16)
        convext = sb("convext", [128, 2 * 520], F32)
        wp16 = sb("wp16", [128, 4 * D], BF16)
        wpw2 = sb("wpw2", [128, 4 * IE], BF16)
        halo = sb("halo", [128, 17 * 3], F32)
        dtb = sb("dtb", [NH, CHUNK], F32)
        ub = sb("ub", [NH, CHUNK], F32)
        bc_buf = sb("bc_buf", [64, CHUNK], BF16)
        c0b = sb("c0b", [NS, CHUNK], BF16)
        state = sb("state", [NS, NH * PH], F32)
        state_bf = sb("state_bf", [NS, NH * PH], BF16)
        XTb = sb("XTb", [128, KT * 128], BF16)
        ystage = sb("ystage", [128, KT * 128], F32)
        ysgbuf = sb("ysgbuf", [128, KT * 128], BF16)
        mtb = sb("mtb", [128, 2 * 128], BF16)
        ceb = sb("ceb", [NS, 2 * 128], BF16)
        bwb = sb("bwb", [128, 2 * NS], BF16)
        btb = sb("btb", [128, NS], BF16)
        s_u = sb("s_u", [NH, Q], F32)
        sTt = sb("sTt", [128, NS], F32)
        nsT = sb("nsT", [128, NS], F32)
        esh = sb("esh", [NH, Q], BF16)
        wdb = sb("wdb", [NH, Q], F32)
        wdb2 = sb("wdb2", [NH, Q], F32)
        WdT = sb("WdT", [128, NS], F32)
        dtT = sb("dtT", [128, NS], F32)
        etotc = sb("etotc", [NH, 1], F32)
        etr = sb("etr", [1, NH], F32)
        etba = sb("etba", [NS, NH], F32)
        scob = sb("scob", [128, 4 * NE], F32)
        cmbw = sb("cmbw", [128, 96], F32)
        combT = sb("combT", [NE, CHUNK], BF16)
        rowbuf = sb("rowbuf", [1, 2 * CHUNK + 128], F32)
        rowb_b = sb("rowb_b", [1, 2 * CHUNK + 128], BF16)
        epsc = sb("epsc_s", [128, 2], F32)
        id_f = sb("id_f_s", [128, 128], F32)
        id_b = sb("id_b_s", [128, 128], BF16)
        negmask_t = sb("negmask_t_s", [128, 128], F32)
        onehot16 = sb("onehot16_s", [NE, NE * 128], BF16)
        rowsel = sb("rowsel_s", [NS, NH * NS], BF16)
        ones_cb = sb("ones_cb", [128, 1], BF16)
        ones_rb = sb("ones_rb", [1, 128], BF16)
        ones_rf = sb("ones_rf", [1, NS], F32)
        gw_b = sb("gw_b", [128, KT * NE], BF16)
        b_in_s = sb("b_in_s", [128, MT_PROJ], F32)
        conv_w_s = sb("conv_w_s", [128, 17 * KC], F32)
        conv_b_s = sb("conv_b_s", [128, 17], F32)
        a_log_s = sb("a_log_s", [NS, 1], F32)
        negA = sb("negA", [NS, 1], F32)
        dt_bias_s = sb("dt_bias_s", [NH, 1], F32)
        d_rep_s = sb("d_rep_s", [128, KT], F32)
        norm_w_s = sb("norm_w_s", [128, KT], F32)

        pbF = [psb(f"pbF{i}", [128, 512], F32) for i in range(6)]
        pbB = [psb(f"pbB{i}", [128, 1024], BF16) for i in range(2)]

        P = Prog(nc, st)
        E = P.emit

        def K_res(i):
            return ("res", i)

        def K_xn(i):
            return ("xn", i)

        def K_pb(i):
            return ("pb", i)

        def K_pbB(i):
            return ("pB", i)

        xn_t = [xn[:, i * CHUNK:(i + 1) * CHUNK] for i in range(KT)]
        res_t = [res[:, i * CHUNK:(i + 1) * CHUNK] for i in range(KT)]
        sg_t = [ubuf[:, SG_O + i * CHUNK:SG_O + (i + 1) * CHUNK] for i in range(KT)]
        xs_t = [ubuf[:, XS_O + i * CHUNK:XS_O + (i + 1) * CHUNK] for i in range(KT)]
        yn_t = [ubuf[:, YN_O + i * CHUNK:YN_O + (i + 1) * CHUNK] for i in range(KT)]
        stg_t = [stg[:, i * CHUNK:(i + 1) * CHUNK] for i in range(3)]

        def warm(n=1, bank=4, col=0, ncol=CHUNK):
            for _ in range(n):
                E("PE", lambda e, bank=bank, col=col, ncol=ncol: e.matmul(
                    pbF[bank][:, col:col + ncol], onehot16[:, 0:128],
                    onehot16[:, 0:ncol], start=True, stop=True),
                  reads=[("cst", "onehot16")], writes=[K_pb(bank)])

        ATTN_PFX = ("sg", "xs", "yn")
        FFN_PFX = ("g", "sh")

        for name, dst, src in [
            ("epsc", epsc, epsc_d), ("id_f", id_f, id_f_d), ("id_b", id_b, id_b_d),
            ("negmaskT", negmask_t, negmask_t_d),
            ("onehot16", onehot16, onehot16_d), ("rowsel", rowsel, rowsel_d),
            ("b_in", b_in_s, b_in_c),
            ("conv_w", conv_w_s, conv_w_c), ("conv_b", conv_b_s, conv_b_c),
            ("a_log", a_log_s, a_log_c), ("dt_bias", dt_bias_s, dt_bias_c),
            ("d_rep", d_rep_s, d_rep_c), ("norm_w", norm_w_s, norm_w_c),
        ]:
            E("SY", lambda e, d_=dst, s_=src: e.dma_start(d_[:, :], s_[:, :]),
              reads=[("cstch",)], writes=[("cst", name), ("cstch",)],
              dma="cst")
        E("DVE", lambda e: e.memset(ones_cb[:], 1.0), writes=[("cst", "ones_cb")])
        E("DVE", lambda e: e.memset(ones_rb[:], 1.0), writes=[("cst", "ones_rb")])
        E("DVE", lambda e: e.memset(ones_rf[:], 1.0), writes=[("cst", "ones_rf")])
        E("DVE", lambda e: e.memset(halo[:], 0.0),
          writes=[("halo", ct) for ct in range(17)])
        E("DVE", lambda e: e.memset(state[:], 0.0),
          writes=[("st", h) for h in range(NH)])
        E("DVE", lambda e: e.memset(state_bf[:], 0.0), writes=[("stb",)])
        E("SY", lambda e: e.dma_start(stg[:, 0:KT * NE], gwt[:, :]),
          reads=[("cstch",)], writes=[("stg", 0), ("cstch",)], dma="cst")
        E("DVE", lambda e: e.tensor_copy(gw_b[:], stg[:, 0:KT * NE]),
          reads=[("stg", 0)], writes=[("cst", "gw_b")])
        E("ACT", lambda e: e.activation(negA[:], a_log_s[:], AF.Exp),
          reads=[("cst", "a_log")], writes=[("negA0",)])
        E("DVE", lambda e: e.tensor_scalar_mul(negA[:], negA[:], -1.0),
          reads=[("negA0",)], writes=[("cst", "negA")])


        scr_key = {id(s_w_in): "w_in", id(s_w_out): "w_out", id(s_ws1): "ws1",
                   id(s_ws3): "ws3", id(s_ws2): "ws2"}
        wslots = [wp16[:, i * D:(i + 1) * D] for i in range(4)]
        wskeys = [("wp16", i) for i in range(4)]
        w2slots = [wpw2[:, i * IE:(i + 1) * IE] for i in range(4)]
        w2keys = [("wpw2", i) for i in range(4)]

        c0ctr = [0]
        c0wb = [0]

        def c0_load(slot, skey, src_ap, ncols):
            i = c0ctr[0]
            c0ctr[0] += 1
            half = i % 2
            st_sl = xstg[:, half * D:half * D + ncols]
            E("SY" if half == 0 else "SC",
              lambda e, st_sl=st_sl, src_ap=src_ap: e.dma_start(st_sl, src_ap),
              writes=[("xstg", half)], dma=("c0a", "c0b")[half])
            if half == 0:
                E("DVE", lambda e, slot=slot, st_sl=st_sl, ncols=ncols:
                  e.tensor_copy(slot[:, 0:ncols], st_sl),
                  reads=[("xstg", half)], writes=[skey])
            else:
                E("ACT", lambda e, slot=slot, st_sl=st_sl, ncols=ncols:
                  e.copy(slot[:, 0:ncols], st_sl),
                  reads=[("xstg", half)], writes=[skey])

        def c0_wb(scr_slice, slot, skey, scrkey):
            i = c0wb[0]
            c0wb[0] += 1
            E("SY" if i % 2 == 0 else "SC",
              lambda e, d_=scr_slice, s_=slot: e.dma_start(d_, s_),
              reads=[skey], writes=[scrkey],
              dma=("sw0", "sw1", "sw2", "sw3")[i % 4] if i % 2 == 0
              else ("sv0", "sv1", "sv2", "sv3")[i % 4])

        def mm_stream(scr_ap, Mt, Ktn, rhs_tiles, rhs_keys, evac,
                      orig_ap=None, first=False, ldeng="SY",
                      slots=(0, 1), dmas=("wp0", "wp1"), c0gp=False):
            sname = scr_key[id(scr_ap)]
            ns = len(slots)
            for m in range(Mt):
                si = slots[m % ns]
                slot, skey = wslots[si], wskeys[si]
                if first:
                    if c0gp:
                        E("GP", lambda e, s_=slot, mm=m, a=orig_ap: e.dma_start(
                            s_, a[mm, :, :]),
                          writes=[skey], dma=f"g{m % 2}")
                    else:
                        c0_load(slot, skey, orig_ap[m, :, :], D)
                else:
                    E(ldeng, lambda e, s_=slot, mm=m, a=scr_ap: e.dma_start(
                        s_, a[mm, :, :]),
                      reads=[("scr", sname, m)], writes=[skey],
                      dma=dmas[m % ns])
                bank = pbF[m % 2]
                bkey = K_pb(m % 2)

                def mmfn(e, s_=slot, bank=bank, Ktn=Ktn):
                    ii = None
                    for k in range(Ktn):
                        ii = e.matmul(bank[:, 0:CHUNK],
                                      s_[:, k * 128:(k + 1) * 128], rhs_tiles[k],
                                      start=(k == 0), stop=(k == Ktn - 1))
                    return ii
                E("PE", mmfn, reads=[skey] + list(rhs_keys), writes=[bkey])
                if first:
                    c0_wb(scr_ap[m, :, :], slot, skey, ("scr", sname, m))
                evac(m, bank, bkey)

        def layernorm(src_keys, epscol):
            for i in range(KT):
                E("ACT", lambda e, i=i: e.copy(xn_t[i], res_t[i]),
                  reads=[src_keys[i]], writes=[K_xn(i)])
                E("DVE", lambda e, i=i: e.tensor_mul(
                    sqb[:, (i % 2) * CHUNK:(i % 2 + 1) * CHUNK],
                    xn_t[i], xn_t[i]),
                  reads=[K_xn(i)], writes=[("sqb", i % 2)])
                E("PE", lambda e, i=i: e.matmul(
                    pbF[2][0:1, 0:CHUNK], ones_cb[:], xn_t[i],
                    start=(i == 0), stop=(i == KT - 1)),
                  reads=[K_xn(i), ("cst", "ones_cb")], writes=[K_pb(2)])
                E("PE", lambda e, i=i: e.matmul(
                    pbF[3][0:1, 0:CHUNK], ones_cb[:],
                    sqb[:, (i % 2) * CHUNK:(i % 2 + 1) * CHUNK],
                    start=(i == 0), stop=(i == KT - 1)),
                  reads=[("sqb", i % 2), ("cst", "ones_cb")], writes=[K_pb(3)])
            m2 = rowbuf[0:1, CHUNK:2 * CHUNK]
            vrow = m2
            E("ACT", lambda e: e.activation(m2, pbF[2][0:1, 0:CHUNK], AF.Square,
                                            scale=1.0 / D),
              reads=[K_pb(2)], writes=[("vrow",)])
            E("ACT", lambda e: e.activation(rowb_b[0:1, 0:CHUNK],
                                            pbF[2][0:1, 0:CHUNK], AF.Copy,
                                            scale=1.0 / D),
              reads=[K_pb(2)], writes=[("mb_b",)])
            warm(2)
            E("DVE", lambda e: e.scalar_tensor_tensor(
                vrow, pbF[3][0:1, 0:CHUNK], 1.0 / D, m2, ALU.mult, ALU.subtract),
              reads=[K_pb(3), ("vrow",)], writes=[("vrow",)])
            E("ACT", lambda e: e.activation(vrow, vrow, AF.Ln,
                                            bias=epsc[0:1, epscol:epscol + 1]),
              reads=[("vrow",), ("cst", "epsc")], writes=[("vrow",)])
            warm(2)
            E("ACT", lambda e: e.activation(vrow, vrow, AF.Exp, scale=-0.5),
              reads=[("vrow",)], writes=[("vrow",)])
            E("DVE", lambda e: e.tensor_copy(rowb_b[0:1, CHUNK:2 * CHUNK], vrow),
              reads=[("vrow",)], writes=[("rr_b",)])
            warm(2)
            E("PE", lambda e: e.matmul(pbF[4][:, 0:CHUNK], ones_rb[:],
                                       rowb_b[0:1, 0:CHUNK],
                                       start=True, stop=True),
              reads=[("mb_b",), ("cst", "ones_rb")], writes=[K_pb(4)])
            E("PE", lambda e: e.matmul(pbF[5][:, 0:CHUNK], ones_rb[:],
                                       rowb_b[0:1, CHUNK:2 * CHUNK],
                                       start=True, stop=True),
              reads=[("rr_b",), ("cst", "ones_rb")], writes=[K_pb(5)])
            for i in range(KT):
                s_ = stg_t[i % 2]
                E("DVE", lambda e, i=i, s_=s_: e.tensor_sub(
                    s_, res_t[i], pbF[4][:, 0:CHUNK]),
                  reads=[src_keys[i], K_pb(4)], writes=[("stg", i % 2)])
                E("DVE", lambda e, i=i, s_=s_: e.tensor_mul(
                    xn_t[i], s_, pbF[5][:, 0:CHUNK]),
                  reads=[("stg", i % 2), K_pb(5)], writes=[K_xn(i)])

        for c in range(n_chunks):
            t0 = c * CHUNK

            P.phase = f"xload"
            for tt in range(NQ):
                sl = xstg[:, (tt % 2) * D:(tt % 2 + 1) * D]
                E("SC", lambda e, tt=tt, sl=sl: e.dma_start(
                    sl, x[t0 + tt * 128:t0 + (tt + 1) * 128, :]),
                  writes=[("xstg", tt % 2)], dma=f"xld{tt % 2}")
                for dj in range(KT):
                    bank = pbF[dj % 2]
                    E("PE", lambda e, sl=sl, dj=dj, bank=bank: e.transpose(
                        bank[:, (dj % 4) * 128:(dj % 4 + 1) * 128],
                        sl[:, dj * 128:(dj + 1) * 128], id_f[:]),
                      reads=[("xstg", tt % 2), ("cst", "id_f")],
                      writes=[K_pb(dj % 2)])
                    if dj % 2 == 0:
                        E("DVE", lambda e, dj=dj, tt=tt, bank=bank: e.tensor_copy(
                            res_t[dj][:, tt * 128:(tt + 1) * 128],
                            bank[:, (dj % 4) * 128:(dj % 4 + 1) * 128]),
                          reads=[K_pb(dj % 2)], writes=[K_res(dj)])
                    else:
                        E("ACT", lambda e, dj=dj, tt=tt, bank=bank: e.copy(
                            res_t[dj][:, tt * 128:(tt + 1) * 128],
                            bank[:, (dj % 4) * 128:(dj % 4 + 1) * 128]),
                          reads=[K_pb(dj % 2)], writes=[K_res(dj)])

            P.phase = "LN1"
            layernorm([K_res(i) for i in range(KT)], 0)

            def proj_evac(m, bank, bkey):
                if m < KT:
                    E("ACT", lambda e, m=m, bank=bank: e.activation(
                        sg_t[m], bank[:, 0:CHUNK], AF.Silu,
                        bias=b_in_s[:, m:m + 1]),
                      reads=[bkey, ("cst", "b_in")], writes=[("sg", m)],
                      conflicts=FFN_PFX if m == 0 else ())
                    return
                ct = m - KT
                rows = 128 if ct < 16 else 64
                ce = convext[:, (ct % 2) * 520:(ct % 2) * 520 + 520]
                E("DVE", lambda e, ce=ce, ct=ct, rows=rows: e.tensor_copy(
                    ce[0:rows, 0:3], halo[0:rows, ct * 3:ct * 3 + 3]),
                  reads=[("halo", ct)], writes=[("cej", ct % 2)])
                E("ACT", lambda e, ce=ce, bank=bank, rows=rows, m=m:
                  e.activation(ce[0:rows, 3:515], bank[0:rows, 0:CHUNK],
                               AF.Identity, bias=b_in_s[0:rows, m:m + 1]),
                  reads=[bkey, ("cst", "b_in")], writes=[("ce2", ct % 2)])
                E("DVE", lambda e, ce=ce, ct=ct, rows=rows: e.tensor_copy(
                    halo[0:rows, ct * 3:ct * 3 + 3], ce[0:rows, 512:515]),
                  reads=[("ce2", ct % 2)], writes=[("halo", ct)])
                acc = stg_t[ct % 2]
                E("DVE", lambda e, ce=ce, ct=ct, rows=rows, acc=acc:
                  e.tensor_scalar_mul(acc[0:rows, :], ce[0:rows, 0:512],
                                      conv_w_s[0:rows, ct * 4:ct * 4 + 1]),
                  reads=[("cej", ct % 2), ("ce2", ct % 2), ("cst", "conv_w")],
                  writes=[("stg", ct % 2)])
                for k in range(1, KC):
                    E("DVE", lambda e, ce=ce, ct=ct, k=k, rows=rows, acc=acc:
                      e.scalar_tensor_tensor(
                          acc[0:rows, :], ce[0:rows, k:k + 512],
                          conv_w_s[0:rows, ct * 4 + k:ct * 4 + k + 1],
                          acc[0:rows, :], ALU.mult, ALU.add),
                      reads=[("cej", ct % 2), ("ce2", ct % 2), ("stg", ct % 2),
                             ("cst", "conv_w")],
                      writes=[("stg", ct % 2)])
                if ct < 16:
                    E("ACT", lambda e, ct=ct, acc=acc: e.activation(
                        xs_t[ct], acc, AF.Silu, bias=conv_b_s[:, ct:ct + 1]),
                      reads=[("stg", ct % 2), ("cst", "conv_b")],
                      writes=[("xs", ct)])
                else:
                    E("ACT", lambda e, acc=acc, ct=ct: e.activation(
                        bc_buf[:, :], acc[0:64, :], AF.Silu,
                        bias=conv_b_s[0:64, ct:ct + 1]),
                      reads=[("stg", ct % 2), ("cst", "conv_b")],
                      writes=[("bc",)])
                    E("ACT", lambda e, bank=bank, m=m: e.activation(
                        stg_t[1][64:96, 0:CHUNK], bank[64:96, 0:CHUNK],
                        AF.Identity, bias=b_in_s[64:96, m:m + 1]),
                      reads=[bkey, ("cst", "b_in")], writes=[("stg", 1)])
                    E("SY", lambda e: e.dma_start(dtb[:, :],
                                                  stg_t[1][64:96, 0:CHUNK]),
                      reads=[("stg", 1)], writes=[("dtraw",)], dma="dtb")
                    E("SY", lambda e: e.dma_start(c0b[:, :],
                                                  bc_buf[32:64, :]),
                      reads=[("bc",)], writes=[("c0",)], dma="c0")
                    E("ACT", lambda e: e.activation(
                        ub[:, :], dtb[:, :], AF.Exp, bias=dt_bias_s[:, :]),
                      reads=[("dtraw",), ("cst", "dt_bias")], writes=[("ue",)])
                    E("ACT", lambda e: e.activation(dtb[:, :], ub[:, :], AF.Ln,
                                                    bias=1.0),
                      reads=[("ue",)], writes=[("dt",)])
                    E("DVE", lambda e: e.tensor_scalar_mul(ub[:, :], dtb[:, :],
                                                           negA[:, :]),
                      reads=[("dt",), ("cst", "negA")], writes=[("u",)])

            P.phase = "proj"
            mm_stream(s_w_in, MT_PROJ, KT, xn_t,
                      [K_xn(i) for i in range(KT)], proj_evac,
                      orig_ap=w_in, first=(c == 0), ldeng="SY",
                      slots=(0, 1), dmas=("wp0", "wp1"))

            for q in range(NQ):
                P.phase = f"scan{q}"
                qs = q * 128
                dtq = dtb[:, qs:qs + 128]
                Bq = bc_buf[0:32, qs:qs + 128]
                Cq = c0b[:, qs:qs + 128]
                E("PE", lambda e, Bq=Bq: e.transpose(pbB[1][:, 896:896 + NS],
                                                     Bq, id_b[0:NS, 0:NS]),
                  reads=[("bc",), ("cst", "id_b")], writes=[K_pbB(1)])
                E("ACT", lambda e: e.copy(btb[:, :],
                                          pbB[1][:, 896:896 + NS]),
                  reads=[K_pbB(1)], writes=[("BT",)])
                E("PE", lambda e, Bq=Bq, Cq=Cq: e.matmul(
                    pbF[3][:, 0:128], Bq, Cq, start=True, stop=True),
                  reads=[("bc",), ("c0",)], writes=[K_pb(3)])
                for ht in range(KT):
                    blk = ht % 8
                    E("PE", lambda e, ht=ht, blk=blk, qs=qs: e.transpose(
                        pbB[0][:, blk * 128:(blk + 1) * 128],
                        xs_t[ht][:, qs:qs + 128], id_b[:]),
                      reads=[("xs", ht), ("cst", "id_b")], writes=[K_pbB(0)])
                    if ht % 2 == 0:
                        E("DVE", lambda e, ht=ht, blk=blk: e.tensor_copy(
                            XTb[:, ht * 128:(ht + 1) * 128],
                            pbB[0][:, blk * 128:(blk + 1) * 128]),
                          reads=[K_pbB(0)], writes=[("XT", ht)])
                    else:
                        E("ACT", lambda e, ht=ht, blk=blk: e.copy(
                            XTb[:, ht * 128:(ht + 1) * 128],
                            pbB[0][:, blk * 128:(blk + 1) * 128]),
                          reads=[K_pbB(0)], writes=[("XT", ht)])

                warm(2, bank=2)
                E("DVE", lambda e, qs=qs: e.tensor_tensor_scan(
                    s_u[:, :], ub[:, qs:qs + 128], ub[:, qs:qs + 128], 0.0,
                    ALU.add, ALU.bypass),
                  reads=[("u",)], writes=[("s_u",)])
                E("PE", lambda e: e.transpose(pbF[0][:, 0:NS], s_u[:, :],
                                              id_f[0:NS, 0:NS]),
                  reads=[("s_u",), ("cst", "id_f")], writes=[K_pb(0)])
                E("ACT", lambda e: e.copy(sTt[:, :], pbF[0][:, 0:NS]),
                  reads=[K_pb(0)], writes=[("sT",)])
                E("DVE", lambda e: e.tensor_scalar_mul(nsT[:, :], sTt[:, :],
                                                       -1.0),
                  reads=[("sT",)], writes=[("nsT",)])
                E("ACT", lambda e: e.activation(esh[:, :], s_u[:, :], AF.Exp),
                  reads=[("s_u",)], writes=[("esh",)])
                E("DVE", lambda e: e.tensor_scalar_sub(
                    wdb[:, :], s_u[:, :], s_u[:, Q - 1:Q]),
                  reads=[("s_u",)], writes=[("wdb",)])
                E("ACT", lambda e: e.activation(wdb2[:, :], wdb[:, :], AF.Exp,
                                                scale=-1.0),
                  reads=[("wdb",)], writes=[("wdb2",)])
                E("DVE", lambda e, dtq=dtq: e.tensor_mul(wdb[:, :], wdb2[:, :],
                                                         dtq),
                  reads=[("wdb2",), ("dt",)], writes=[("wd",)])
                E("PE", lambda e: e.transpose(pbF[0][:, 64:64 + NS], wdb[:, :],
                                              id_f[0:NS, 0:NS]),
                  reads=[("wd",), ("cst", "id_f")], writes=[K_pb(0)])
                E("ACT", lambda e: e.copy(WdT[:, :],
                                          pbF[0][:, 64:64 + NS]),
                  reads=[K_pb(0)], writes=[("WdT",)])
                E("PE", lambda e, dtq=dtq: e.transpose(
                    pbF[0][:, 128:128 + NS], dtq, id_f[0:NS, 0:NS]),
                  reads=[("dt",), ("cst", "id_f")], writes=[K_pb(0)])
                E("ACT", lambda e: e.copy(dtT[:, :],
                                          pbF[0][:, 128:128 + NS]),
                  reads=[K_pb(0)], writes=[("dtT",)])
                E("ACT", lambda e: e.activation(etotc[:, :], s_u[:, Q - 1:Q],
                                                AF.Exp),
                  reads=[("s_u",)], writes=[("etotc",)])
                E("PE", lambda e: e.transpose(pbF[0][0:1, 192:192 + NS],
                                              etotc[:, :], id_f[0:NS, 0:NS]),
                  reads=[("etotc",), ("cst", "id_f")], writes=[K_pb(0)])
                E("ACT", lambda e: e.copy(
                    etr[:, :], pbF[0][0:1, 192:192 + NS]),
                  reads=[K_pb(0)], writes=[("etr",)])
                E("PE", lambda e: e.matmul(
                    pbF[0][0:NS, 224:224 + NH], ones_rf[:, :], etr[:, :],
                    start=True, stop=True),
                  reads=[("etr",), ("cst", "ones_rf")], writes=[K_pb(0)])
                E("ACT", lambda e: e.copy(etba[:, :], pbF[0][0:NS, 224:224 + NH]),
                  reads=[K_pb(0)], writes=[("etba",)])
                def stageA(h):
                    hb = h % 2
                    h4 = h % 3
                    bank = pbF[h4]
                    colb = stg_t[2][:, hb * 128:(hb + 1) * 128]
                    E("ACT", lambda e, h=h, colb=colb: e.activation(
                        colb, negmask_t[:], AF.Identity, bias=sTt[:, h:h + 1]),
                      reads=[("sT",), ("cst", "negmaskT")],
                      writes=[("colb", hb)])
                    E("PE", lambda e, colb=colb, bank=bank: e.transpose(
                        bank[:, 0:128], colb, id_f[:]),
                      reads=[("colb", hb), ("cst", "id_f")], writes=[K_pb(h4)])
                    E("PE", lambda e, h=h, bank=bank: e.matmul(
                        bank[0:NS, 256:384], rowsel[:, h * NS:(h + 1) * NS],
                        esh[:, :], start=True, stop=True),
                      reads=[("esh",), ("cst", "rowsel")], writes=[K_pb(h4)])
                    warm(1, bank=h4, col=128, ncol=128)

                def stageB(h):
                    hb = h % 2
                    h4 = h % 3
                    bank = pbF[h4]
                    sLh = stg_t[2][:, 256 + hb * 128:256 + (hb + 1) * 128]
                    E("ACT", lambda e, h=h, sLh=sLh, bank=bank: e.activation(
                        sLh, bank[:, 0:128], AF.Exp, bias=nsT[:, h:h + 1]),
                      reads=[K_pb(h4), ("nsT",)], writes=[("sLh", hb)])
                    E("DVE", lambda e, h=h, hb=hb, sLh=sLh:
                      e.scalar_tensor_tensor(
                          mtb[:, hb * 128:(hb + 1) * 128], sLh,
                          dtT[:, h:h + 1], pbF[3][:, 0:128],
                          ALU.mult, ALU.mult),
                      reads=[("sLh", hb), ("dtT",), K_pb(3)],
                      writes=[("mt", hb)])
                    E("DVE", lambda e, hb=hb, bank=bank, Cq=Cq: e.tensor_mul(
                        ceb[:, hb * 128:(hb + 1) * 128],
                        bank[0:NS, 256:384], Cq),
                      reads=[K_pb(h4), ("c0",)], writes=[("ceh", hb)])
                    E("ACT", lambda e, h=h, hb=hb: e.activation(
                        bwb[:, hb * NS:(hb + 1) * NS], btb[:, :], AF.Copy,
                        scale=WdT[:, h:h + 1]),
                      reads=[("BT",), ("WdT",)], writes=[("bw", hb)])

                def stageCD(p):
                    h0, h1 = 2 * p, 2 * p + 1
                    yb = pbF[4 + (p % 2)]
                    ybk = K_pb(4 + (p % 2))

                    def ygrp(e, yb=yb, h0=h0, h1=h1):
                        ii = e.matmul(yb[0:64, 0:128],
                                      XTb[:, h0 * 64:(h0 + 1) * 64],
                                      mtb[:, 0:128], start=True, stop=False)
                        ii = e.matmul(yb[0:64, 0:128],
                                      state_bf[:, h0 * 64:(h0 + 1) * 64],
                                      ceb[:, 0:128], start=False, stop=True)
                        ii = e.matmul(yb[64:128, 0:128],
                                      XTb[:, h1 * 64:(h1 + 1) * 64],
                                      mtb[:, 128:256], start=True, stop=False)
                        ii = e.matmul(yb[64:128, 0:128],
                                      state_bf[:, h1 * 64:(h1 + 1) * 64],
                                      ceb[:, 128:256], start=False, stop=True)
                        return ii
                    E("PE", ygrp,
                      reads=[("XT", p), ("mt", 0), ("mt", 1), ("ceh", 0),
                             ("ceh", 1), ("stb",)],
                      writes=[ybk])
                    E("PE", lambda e, yb=yb, h0=h0: e.matmul(
                        yb[0:NS, 384:448], bwb[:, 0:NS],
                        XTb[:, h0 * 64:(h0 + 1) * 64], start=True, stop=True),
                      reads=[("bw", 0), ("XT", p)], writes=[ybk])
                    E("PE", lambda e, yb=yb, h1=h1: e.matmul(
                        yb[0:NS, 448:512], bwb[:, NS:2 * NS],
                        XTb[:, h1 * 64:(h1 + 1) * 64], start=True, stop=True),
                      reads=[("bw", 1), ("XT", p)], writes=[ybk])
                    E("DVE", lambda e, p=p, yb=yb, qs=qs: e.scalar_tensor_tensor(
                        ystage[:, p * 128:(p + 1) * 128],
                        xs_t[p][:, qs:qs + 128], d_rep_s[:, p:p + 1],
                        yb[:, 0:128], ALU.mult, ALU.add),
                      reads=[("xs", p), ("cst", "d_rep"), ybk],
                      writes=[("yst", p)])
                    E("DVE", lambda e, h0=h0, yb=yb: e.scalar_tensor_tensor(
                        state[:, h0 * 64:(h0 + 1) * 64],
                        state[:, h0 * 64:(h0 + 1) * 64], etba[:, h0:h0 + 1],
                        yb[0:NS, 384:448], ALU.mult, ALU.add),
                      reads=[("st", h0), ("etba",), ybk], writes=[("st", h0)])
                    E("DVE", lambda e, h1=h1, yb=yb: e.scalar_tensor_tensor(
                        state[:, h1 * 64:(h1 + 1) * 64],
                        state[:, h1 * 64:(h1 + 1) * 64], etba[:, h1:h1 + 1],
                        yb[0:NS, 448:512], ALU.mult, ALU.add),
                      reads=[("st", h1), ("etba",), ybk], writes=[("st", h1)])

                stageA(0)
                stageA(1)
                stageA(2)
                for h in range(NH):
                    stageB(h)
                    if h + 3 < NH:
                        stageA(h + 3)
                    if h % 2 == 1:
                        stageCD(h // 2)

                E("DVE", lambda e: e.tensor_copy(state_bf[:, :], state[:, :]),
                  reads=[("st", h) for h in range(NH)], writes=[("stb",)])

                P.phase = f"mixtail{q}"
                for ht in range(KT):
                    E("DVE", lambda e, ht=ht, qs=qs: e.tensor_mul(
                        ysgbuf[:, ht * 128:(ht + 1) * 128],
                        ystage[:, ht * 128:(ht + 1) * 128],
                        sg_t[ht][:, qs:qs + 128]),
                      reads=[("yst", ht), ("sg", ht)], writes=[("ysgb", ht)])
                    E("ACT", lambda e, ht=ht: e.activation(
                        sqb[:, (ht % 4) * 256:(ht % 4) * 256 + 128],
                        ysgbuf[:, ht * 128:(ht + 1) * 128], AF.Square),
                      reads=[("ysgb", ht)], writes=[("ysq", ht % 4)])
                    E("PE", lambda e, ht=ht: e.matmul(
                        pbF[0][0:1, 256:384], ones_cb[:],
                        sqb[:, (ht % 4) * 256:(ht % 4) * 256 + 128],
                        start=(ht == 0), stop=(ht == KT - 1)),
                      reads=[("ysq", ht % 4), ("cst", "ones_cb")],
                      writes=[K_pb(0)])
                warm(3, bank=2)
                rr = rowbuf[0:1, 2 * CHUNK:2 * CHUNK + 128]
                E("ACT", lambda e, rr=rr: e.activation(
                    rr, pbF[0][0:1, 256:384], AF.Ln, bias=epsc[0:1, 1:2],
                    scale=1.0 / D),
                  reads=[K_pb(0), ("cst", "epsc")], writes=[("rmsr",)])
                E("ACT", lambda e, rr=rr: e.activation(rr, rr, AF.Exp,
                                                       scale=-0.5),
                  reads=[("rmsr",)], writes=[("rmsr",)])
                E("DVE", lambda e, rr=rr: e.tensor_copy(
                    rowb_b[0:1, 2 * CHUNK:2 * CHUNK + 128], rr),
                  reads=[("rmsr",)], writes=[("rmsb",)])
                E("PE", lambda e: e.matmul(
                    pbF[1][:, 256:384], ones_rb[:],
                    rowb_b[0:1, 2 * CHUNK:2 * CHUNK + 128],
                    start=True, stop=True),
                  reads=[("rmsb",), ("cst", "ones_rb")], writes=[K_pb(1)])
                for ht in range(KT):
                    E("DVE", lambda e, ht=ht, qs=qs: e.scalar_tensor_tensor(
                        yn_t[ht][:, qs:qs + 128],
                        ysgbuf[:, ht * 128:(ht + 1) * 128],
                        norm_w_s[:, ht:ht + 1], pbF[1][:, 256:384],
                        ALU.mult, ALU.mult),
                      reads=[("ysgb", ht), ("cst", "norm_w"), K_pb(1)],
                      writes=[("yn", ht, q)])

            yn_keys = [("yn", i, qq) for i in range(KT) for qq in range(NQ)]

            def wout_evac(m, bank, bkey):
                E("DVE", lambda e, m=m, bank=bank: e.tensor_add(
                    res_t[m], res_t[m], bank[:, 0:CHUNK]),
                  reads=[K_res(m), bkey], writes=[K_res(m)])

            P.phase = "wout"
            mm_stream(s_w_out, KT, KT, yn_t, yn_keys, wout_evac,
                      orig_ap=w_out, first=(c == 0), ldeng="GP",
                      slots=(2, 3), dmas=("g0", "g1"), c0gp=True)

            P.phase = "LN2"
            layernorm([K_res(i) for i in range(KT)], 0)

            P.phase = "route"
            for tt in range(NQ):
                def scoremm(e, tt=tt):
                    ii = None
                    for k in range(KT):
                        ii = e.matmul(pbF[5][:, tt * NE:(tt + 1) * NE],
                                      xn_t[k][:, tt * 128:(tt + 1) * 128],
                                      gw_b[:, k * NE:(k + 1) * NE],
                                      start=(k == 0), stop=(k == KT - 1))
                    return ii
                E("PE", scoremm,
                  reads=[K_xn(k) for k in range(KT)] + [("cst", "gw_b")],
                  writes=[K_pb(5)])
                warm(2, bank=0)
                sco = scob[:, tt * NE:(tt + 1) * NE]
                E("ACT", lambda e, sco=sco, tt=tt: e.activation(
                    sco, pbF[5][:, tt * NE:(tt + 1) * NE], AF.Exp, scale=-1.0),
                  reads=[K_pb(5)], writes=[("sco", tt)])
                E("DVE", lambda e, sco=sco: e.tensor_scalar_add(sco, sco, 1.0),
                  reads=[("sco", tt)], writes=[("sco", tt)])
                E("DVE", lambda e, sco=sco: e.reciprocal(sco, sco),
                  reads=[("sco", tt)], writes=[("sco", tt)])
                gm4 = cmbw[:, 0:4]
                gmk = cmbw[:, 5:9]
                tcol = cmbw[:, 9:10]
                wsum = cmbw[:, 10:11]
                rws = cmbw[:, 11:12]
                gm4b = cmbw[:, 12:16]
                msk = cmbw[:, 16:32]
                mskw = cmbw[:, 80:96]
                ecol = cmbw[:, 32:48]
                wv = cmbw[:, 48:64]
                comb = cmbw[:, 64:80]
                m1c = cmbw[:, 4:5]
                for g_ in range(4):
                    E("DVE", lambda e, g_=g_, sco=sco: e.reduce_max(
                        gm4[:, g_:g_ + 1], sco[:, 4 * g_:4 * g_ + 4], axis=AX),
                      reads=[("sco", tt)], writes=[("gm4", tt)])
                E("DVE", lambda e: e.reduce_max(m1c, gm4, axis=AX),
                  reads=[("gm4", tt)], writes=[("m1c", tt)])
                E("DVE", lambda e: e.tensor_scalar(
                    gm4b, gm4, m1c, None, ALU.is_equal),
                  reads=[("gm4", tt), ("m1c", tt)], writes=[("gm4b", tt)])
                E("DVE", lambda e: e.scalar_tensor_tensor(
                    gm4b, gm4b, -MBIG, gm4, ALU.mult, ALU.add),
                  reads=[("gm4b", tt), ("gm4", tt)], writes=[("gm4b", tt)])
                E("DVE", lambda e: e.reduce_max(tcol, gm4b, axis=AX),
                  reads=[("gm4b", tt)], writes=[("tcol", tt)])
                E("DVE", lambda e: e.tensor_scalar(
                    gmk, gm4, tcol, None, ALU.is_ge),
                  reads=[("gm4", tt), ("tcol", tt)], writes=[("gmk", tt)])
                for g_ in range(4):
                    E("DVE", lambda e, g_=g_, sco=sco: e.tensor_scalar_mul(
                        msk[:, 4 * g_:4 * g_ + 4], sco[:, 4 * g_:4 * g_ + 4],
                        gmk[:, g_:g_ + 1]),
                      reads=[("sco", tt), ("gmk", tt)], writes=[("msk", tt)])
                E("DVE", lambda e: e.tensor_copy(mskw, msk),
                  reads=[("msk", tt)], writes=[("mskw", tt)])
                for _ in range(3):
                    E("DVE", lambda e: e.reduce_max(tcol, mskw, axis=AX),
                      reads=[("mskw", tt)], writes=[("tcol", tt)])
                    E("DVE", lambda e: e.tensor_scalar(
                        ecol, mskw, tcol, None, ALU.is_equal),
                      reads=[("mskw", tt), ("tcol", tt)], writes=[("ecol", tt)])
                    E("DVE", lambda e: e.scalar_tensor_tensor(
                        mskw, ecol, -MBIG, mskw, ALU.mult, ALU.add),
                      reads=[("ecol", tt), ("mskw", tt)], writes=[("mskw", tt)])
                E("DVE", lambda e: e.reduce_max(tcol, mskw, axis=AX),
                  reads=[("mskw", tt)], writes=[("tcol", tt)])
                E("DVE", lambda e: e.tensor_scalar(
                    ecol, msk, tcol, None, ALU.is_ge),
                  reads=[("msk", tt), ("tcol", tt)], writes=[("ecol", tt)])
                E("DVE", lambda e, sco=sco: e.tensor_mul(wv, sco, ecol),
                  reads=[("ecol", tt), ("sco", tt)], writes=[("wv", tt)])
                E("DVE", lambda e: e.tensor_reduce(wsum, wv, AX, ALU.add),
                  reads=[("wv", tt)], writes=[("wsum", tt)])
                E("DVE", lambda e: e.reciprocal(rws, wsum),
                  reads=[("wsum", tt)], writes=[("rws", tt)])
                E("DVE", lambda e: e.tensor_scalar_mul(comb, wv, rws),
                  reads=[("wv", tt), ("rws", tt)], writes=[("comb", tt)])
                E("PE", lambda e, tt=tt: e.transpose(
                    pbF[4][0:NE, tt * 128:(tt + 1) * 128], comb, id_f[:]),
                  reads=[("comb", tt), ("cst", "id_f")], writes=[K_pb(4)])
                E("DVE", lambda e, tt=tt: e.tensor_copy(
                    combT[:, tt * 128:(tt + 1) * 128],
                    pbF[4][0:NE, tt * 128:(tt + 1) * 128]),
                  reads=[K_pb(4)], writes=[("combT", tt)])
                warm(2, bank=0)

            for e_i in range(NE):
                P.phase = "expert"
                gb = e_i % 2
                for it in range(IT_E):
                    s1, k1 = wslots[it % 2], wskeys[it % 2]
                    s3, k3 = wslots[2 + it % 2], wskeys[2 + it % 2]
                    if c == 0:
                        E("GP", lambda e, s1=s1, e_i=e_i, it=it: e.dma_start(
                            s1, w1[e_i, it, :, :]),
                          writes=[k1], dma=f"g{it % 2}")
                        c0_load(s3, k3, w3[e_i, it, :, :], D)
                        c0_wb(s_w1[e_i, it, :, :], s1, k1,
                              ("scr", "w1", e_i, it))
                        c0_wb(s_w3[e_i, it, :, :], s3, k3,
                              ("scr", "w3", e_i, it))
                    else:
                        E("GP", lambda e, s1=s1, e_i=e_i, it=it: e.dma_start(
                            s1, s_w1[e_i, it, :, :]),
                          reads=[("scr", "w1", e_i, it)], writes=[k1],
                          dma=f"g{2 + it % 2}")
                        E("SY", lambda e, s3=s3, e_i=e_i, it=it: e.dma_start(
                            s3, s_w3[e_i, it, :, :]),
                          reads=[("scr", "w3", e_i, it)], writes=[k3],
                          dma=f"wp{2 + it % 2}")

                    def h1mm(e, s1=s1):
                        ii = None
                        for k in range(KT):
                            ii = e.matmul(pbF[0][:, 0:CHUNK],
                                          s1[:, k * 128:(k + 1) * 128], xn_t[k],
                                          start=(k == 0), stop=(k == KT - 1))
                        return ii
                    E("PE", h1mm, reads=[k1] + [K_xn(k) for k in range(KT)],
                      writes=[K_pb(0)])

                    def h3mm(e, s3=s3):
                        ii = None
                        for k in range(KT):
                            ii = e.matmul(pbF[1][:, 0:CHUNK],
                                          s3[:, k * 128:(k + 1) * 128], xn_t[k],
                                          start=(k == 0), stop=(k == KT - 1))
                        return ii
                    E("PE", h3mm, reads=[k3] + [K_xn(k) for k in range(KT)],
                      writes=[K_pb(1)])
                    if it == 0:
                        E("PE", lambda e, e_i=e_i: e.matmul(
                            pbF[2][:, 0:CHUNK],
                            onehot16[:, e_i * 128:(e_i + 1) * 128],
                            combT[:, :], start=True, stop=True),
                          reads=[("combT", tt) for tt in range(NQ)] +
                                [("cst", "onehot16")],
                          writes=[K_pb(2)])
                    E("ACT", lambda e: e.activation(stg_t[0], pbF[0][:, 0:CHUNK],
                                                    AF.Silu),
                      reads=[K_pb(0)], writes=[("stg", 0)])
                    E("DVE", lambda e: e.tensor_mul(stg_t[1], stg_t[0],
                                                    pbF[1][:, 0:CHUNK]),
                      reads=[("stg", 0), K_pb(1)], writes=[("stg", 1)])
                    E("DVE", lambda e, gb=gb, it=it: e.tensor_mul(
                        ubuf[:, G_O[gb] + it * CHUNK:G_O[gb] + (it + 1) * CHUNK],
                        stg_t[1], pbF[2][:, 0:CHUNK]),
                      reads=[("stg", 1), K_pb(2)], writes=[("g", gb, it)],
                      conflicts=ATTN_PFX if (e_i == 0 and it == 0) else ())
                P.phase = "w2"
                for m in range(KT):
                    sl, kk = w2slots[m % 4], w2keys[m % 4]
                    if c == 0:
                        c0_load(sl, kk, w2[e_i, m, :, :], IE)
                        c0_wb(s_w2[e_i, m, :, :], sl, kk,
                              ("scr", "w2", e_i, m))
                    else:
                        E("SC", lambda e, sl=sl, e_i=e_i, m=m: e.dma_start(
                            sl, s_w2[e_i, m, :, :]),
                          reads=[("scr", "w2", e_i, m)], writes=[kk],
                          dma=f"w2{m % 4}")

                    def w2mm(e, sl=sl, gb=gb, m=m):
                        ii = None
                        for k in range(IT_E):
                            ii = e.matmul(
                                pbF[3 + m % 3][:, 0:CHUNK],
                                sl[:, k * 128:(k + 1) * 128],
                                ubuf[:, G_O[gb] + k * CHUNK:
                                     G_O[gb] + (k + 1) * CHUNK],
                                start=(k == 0), stop=(k == IT_E - 1))
                        return ii
                    E("PE", w2mm,
                      reads=[kk] + [("g", gb, k) for k in range(IT_E)],
                      writes=[K_pb(3 + m % 3)])
                    E("DVE", lambda e, m=m: e.tensor_add(
                        res_t[m], res_t[m], pbF[3 + m % 3][:, 0:CHUNK]),
                      reads=[K_res(m), K_pb(3 + m % 3)], writes=[K_res(m)])

            P.phase = "shared"
            for m in range(MT_SH):
                s1, k1 = wslots[m % 2], wskeys[m % 2]
                s3, k3 = wslots[2 + m % 2], wskeys[2 + m % 2]
                if c == 0:
                    E("GP", lambda e, s1=s1, m=m: e.dma_start(
                        s1, ws1[m, :, :]),
                      writes=[k1], dma=f"g{m % 2}")
                    c0_load(s3, k3, ws3[m, :, :], D)
                    c0_wb(s_ws1[m, :, :], s1, k1, ("scr", "ws1", m))
                    c0_wb(s_ws3[m, :, :], s3, k3, ("scr", "ws3", m))
                else:
                    E("GP", lambda e, s1=s1, m=m: e.dma_start(
                        s1, s_ws1[m, :, :]),
                      reads=[("scr", "ws1", m)], writes=[k1], dma=f"g{2 + m % 2}")
                    E("SY", lambda e, s3=s3, m=m: e.dma_start(
                        s3, s_ws3[m, :, :]),
                      reads=[("scr", "ws3", m)], writes=[k3],
                      dma=f"wp{2 + m % 2}")

                def sh1mm(e, s1=s1):
                    ii = None
                    for k in range(KT):
                        ii = e.matmul(pbF[0][:, 0:CHUNK],
                                      s1[:, k * 128:(k + 1) * 128], xn_t[k],
                                      start=(k == 0), stop=(k == KT - 1))
                    return ii
                E("PE", sh1mm, reads=[k1] + [K_xn(k) for k in range(KT)],
                  writes=[K_pb(0)])

                def sh3mm(e, s3=s3):
                    ii = None
                    for k in range(KT):
                        ii = e.matmul(pbF[1][:, 0:CHUNK],
                                      s3[:, k * 128:(k + 1) * 128], xn_t[k],
                                      start=(k == 0), stop=(k == KT - 1))
                    return ii
                E("PE", sh3mm, reads=[k3] + [K_xn(k) for k in range(KT)],
                  writes=[K_pb(1)])
                E("ACT", lambda e: e.activation(stg_t[0], pbF[0][:, 0:CHUNK],
                                                AF.Silu),
                  reads=[K_pb(0)], writes=[("stg", 0)])
                E("DVE", lambda e, m=m: e.tensor_mul(
                    ubuf[:, SH_O + m * CHUNK:SH_O + (m + 1) * CHUNK],
                    stg_t[0], pbF[1][:, 0:CHUNK]),
                  reads=[("stg", 0), K_pb(1)], writes=[("sh", m)])
            P.phase = "ws2"
            for m in range(KT):
                sa, ka = w2slots[2 + m % 2], w2keys[2 + m % 2]
                sb_, kb = w2slots[m % 2], w2keys[m % 2]
                if c == 0:
                    c0_load(sa, ka, ws2[m, :, 0:IT_E * 128], IT_E * 128)
                    c0_load(sb_, kb, ws2[m, :, IT_E * 128:SHD], SHD - IT_E * 128)
                    c0_wb(s_ws2[m, :, 0:IT_E * 128], sa, ka,
                          ("scr", "ws2", m, 0))
                    c0_wb(s_ws2[m, :, IT_E * 128:SHD], sb_, kb,
                          ("scr", "ws2", m, 1))
                else:
                    E("SC", lambda e, sa=sa, m=m: e.dma_start(
                        sa, s_ws2[m, :, 0:IT_E * 128]),
                      reads=[("scr", "ws2", m, 0)], writes=[ka],
                      dma=f"w2{2 + m % 2}")
                    E("SC", lambda e, sb_=sb_, m=m: e.dma_start(
                        sb_, s_ws2[m, :, IT_E * 128:SHD]),
                      reads=[("scr", "ws2", m, 1)], writes=[kb],
                      dma=f"w2{m % 2}")

                def ws2mm(e, sa=sa, sb_=sb_, m=m):
                    ii = None
                    for k in range(MT_SH):
                        if k < IT_E:
                            lhs = sa[:, k * 128:(k + 1) * 128]
                        else:
                            lhs = sb_[:, (k - IT_E) * 128:(k - IT_E + 1) * 128]
                        ii = e.matmul(pbF[3 + m % 3][:, 0:CHUNK], lhs,
                                      ubuf[:, SH_O + k * CHUNK:
                                           SH_O + (k + 1) * CHUNK],
                                      start=(k == 0), stop=(k == MT_SH - 1))
                    return ii
                E("PE", ws2mm,
                  reads=[ka, kb] + [("sh", k) for k in range(MT_SH)],
                  writes=[K_pb(3 + m % 3)])
                E("DVE", lambda e, m=m: e.tensor_add(
                    res_t[m], res_t[m], pbF[3 + m % 3][:, 0:CHUNK]),
                  reads=[K_res(m), K_pb(3 + m % 3)], writes=[K_res(m)])

            P.phase = "ostore"
            for tt in range(NQ):
                sl = xstg[:, (tt % 2) * D:(tt % 2 + 1) * D]
                for dj in range(KT):
                    bank = pbF[dj % 2]
                    E("PE", lambda e, dj=dj, tt=tt, bank=bank: e.transpose(
                        bank[:, (dj % 4) * 128:(dj % 4 + 1) * 128],
                        res_t[dj][:, tt * 128:(tt + 1) * 128], id_f[:]),
                      reads=[K_res(dj), ("cst", "id_f")], writes=[K_pb(dj % 2)])
                    if dj % 2 == 0:
                        E("DVE", lambda e, dj=dj, sl=sl, bank=bank: e.tensor_copy(
                            sl[:, dj * 128:(dj + 1) * 128],
                            bank[:, (dj % 4) * 128:(dj % 4 + 1) * 128]),
                          reads=[K_pb(dj % 2)],
                          writes=[("xstg", tt % 2, dj), ("xstg", tt % 2)])
                    else:
                        E("ACT", lambda e, dj=dj, sl=sl, bank=bank: e.copy(
                            sl[:, dj * 128:(dj + 1) * 128],
                            bank[:, (dj % 4) * 128:(dj % 4 + 1) * 128]),
                          reads=[K_pb(dj % 2)],
                          writes=[("xstg", tt % 2, dj), ("xstg", tt % 2)])
                E("SC", lambda e, sl=sl, tt=tt: e.dma_start(
                    out[t0 + tt * 128:t0 + (tt + 1) * 128, :], sl),
                  reads=[("xstg", tt % 2, dj) for dj in range(KT)],
                  writes=[("xstg", tt % 2)], dma=f"ost{tt % 2}")

        P.barrier_all()
    build_nc.last_prog = P
    return nc


def kernel(**inputs):
    x = np.asarray(inputs["L0_attn_x"], np.float32)
    d = _prep_inputs(inputs)
    nc = build_nc()
    in_maps = []
    for b in range(N_CORES):
        m = dict(d)
        m["x"] = np.ascontiguousarray(x[b])
        in_maps.append(m)
    res = run_bass_kernel_spmd(nc, in_maps, core_ids=list(range(N_CORES)))
    return np.stack([np.asarray(r["out"], np.float32) for r in res.results])


# version 08
# speedup vs baseline: 1.2152x; 1.2152x over previous
import numpy as np
import ml_dtypes
import concourse.bass as bass
import concourse.mybir as mybir
from concourse.bass_utils import run_bass_kernel_spmd
from contextlib import ExitStack

F32 = mybir.dt.float32
BF16 = mybir.dt.bfloat16
AF = mybir.ActivationFunctionType
ALU = mybir.AluOpType
AX = mybir.AxisListType.X

D = 2048
KT = 16
PROJ = 4192
PROJP = 4224
MT_PROJ = 33
NH = 32
PH = 64
NS = 32
KC = 4
NE = 16
IE = 1408
IT_E = 11
SHD = 2816
MT_SH = 22
L = 4096
CHUNK = 512
NQ = 4
Q = 128
N_CORES = 8

LN_EPS = 1e-5
RMS_EPS = 1e-6
NEGBIG = -1e9
MBIG = 1e4

N_CHUNKS_DEFAULT = L // CHUNK


class Prog:
    STREAM = dict(PE="PE", DVE="DVE", ACT="ACT", SY="SY", SC="ACT", GP="GP")
    DSEM = dict(PE="PE", DVE="DVE", ACT="ACT", SY="SYD", SC="SCD", GP="POOL")

    def __init__(self, nc, stack):
        self.nc = nc
        self.eng = dict(PE=nc.tensor, DVE=nc.vector, ACT=nc.scalar,
                        SY=nc.sync, SC=nc.scalar, GP=nc.gpsimd)
        self.streams = dict(PE=nc.tensor, DVE=nc.vector, ACT=nc.scalar,
                            SY=nc.sync, GP=nc.gpsimd)
        names = ["PE", "DVE", "ACT", "POOL"] + [
            "d:" + s for s in ("wp0", "wp1", "wp2", "wp3", "w20", "w21", "w22",
                               "w23", "xld0", "xld1", "ost0", "ost1", "dtb",
                               "c0", "cst", "prol", "g0", "g1", "g2", "g3",
                               "h0", "h1", "h2", "h3", "sw0", "sw1", "sw2",
                               "sw3", "sv0", "sv1", "sv2", "sv3",
                               "c0a", "c0b")]
        self.sem = {k: stack.enter_context(
            nc.semaphore("S" + k.replace(":", "_"))) for k in names}
        self.cnt = dict.fromkeys(self.sem, 0)
        self.wm = {}
        self.buf = {}
        self.phase = "init"
        self.waitlog = []

    def _wait(self, stream, semname, val):
        key = (stream, semname)
        if self.wm.get(key, 0) >= val:
            return
        self.streams[stream].wait_ge(self.sem[semname], val)
        self.waitlog.append((stream, semname, val, self.phase))
        self.wm[key] = val

    def emit(self, engine, fn, reads=(), writes=(), conflicts=(), dma=None):
        stream = self.STREAM[engine]
        need = {}

        def add(tag):
            if tag is None:
                return
            s, c = tag
            if c > need.get(s, 0):
                need[s] = c

        for k in reads:
            ent = self.buf.get(k)
            if ent is not None:
                add(ent[0])
        for k in writes:
            ent = self.buf.get(k)
            if ent is not None:
                add(ent[0])
                for r in ent[1]:
                    add(r)
        if conflicts:
            dead = [k for k in self.buf
                    if (k[0] if isinstance(k, tuple) else k) in conflicts]
            for k in dead:
                ent = self.buf[k]
                add(ent[0])
                for r in ent[1]:
                    add(r)
                del self.buf[k]
        for s in sorted(need):
            self._wait(stream, s, need[s])
        instr = fn(self.eng[engine])
        if dma is not None:
            semname, step = "d:" + dma, 16
        else:
            semname, step = self.DSEM[engine], 1
        self.cnt[semname] += step
        instr.then_inc(self.sem[semname], step)
        tag = (semname, self.cnt[semname])
        for k in writes:
            self.buf[k] = [tag, []]
        for k in reads:
            ent = self.buf.get(k)
            if ent is not None:
                ent[1].append(tag)
        return tag

    def barrier_all(self):
        for stream in self.streams:
            for s, c in self.cnt.items():
                if c:
                    self._wait(stream, s, c)


def _tiles_mkpj(wT):
    K, M = wT.shape
    t = wT.reshape(K // 128, 128, M // 128, 128)
    return np.ascontiguousarray(t.transpose(2, 1, 0, 3).reshape(M // 128, 128, K))


def _prep_inputs(ip):
    f = np.float32
    W_in = np.asarray(ip["L0_attn_W_in"], f)
    W_in_p = np.zeros((D, PROJP), f)
    W_in_p[:, :PROJ] = W_in
    b_in = np.zeros((PROJP,), f)
    b_in[:PROJ] = np.asarray(ip["L0_attn_b_in"], f)
    conv_w = np.asarray(ip["L0_attn_conv_w"], f)
    conv_wp = np.zeros((17 * 128, KC), f)
    conv_wp[:2112] = conv_w
    conv_b = np.zeros((17 * 128,), f)
    conv_b[:2112] = np.asarray(ip["L0_attn_conv_b"], f)
    W1 = np.asarray(ip["L0_ffn_W1"], f)
    W2 = np.asarray(ip["L0_ffn_W2"], f)
    W3 = np.asarray(ip["L0_ffn_W3"], f)

    d = {}
    d["w_in"] = _tiles_mkpj(W_in_p)
    d["w_out"] = _tiles_mkpj(np.asarray(ip["L0_attn_W_out"], f))
    d["w1"] = np.stack([_tiles_mkpj(W1[e].T) for e in range(NE)])
    d["w3"] = np.stack([_tiles_mkpj(W3[e].T) for e in range(NE)])
    d["w2"] = np.stack([_tiles_mkpj(W2[e].T) for e in range(NE)])
    d["ws1"] = _tiles_mkpj(np.asarray(ip["L0_ffn_Ws1"], f).T)
    d["ws3"] = _tiles_mkpj(np.asarray(ip["L0_ffn_Ws3"], f).T)
    d["ws2"] = _tiles_mkpj(np.asarray(ip["L0_ffn_Ws2"], f).T)
    gwT = np.asarray(ip["L0_ffn_gate_w"], f).T
    d["gwt"] = np.ascontiguousarray(
        gwT.reshape(KT, 128, NE).transpose(1, 0, 2).reshape(128, KT * NE))
    d["b_in_c"] = np.ascontiguousarray(b_in.reshape(MT_PROJ, 128).T)
    d["conv_w_c"] = np.ascontiguousarray(
        conv_wp.reshape(17, 128, KC).transpose(1, 0, 2).reshape(128, 17 * KC))
    d["conv_b_c"] = np.ascontiguousarray(conv_b.reshape(17, 128).T)
    d["a_log_c"] = np.asarray(ip["L0_attn_A_log"], f).reshape(NS, 1)
    d["dt_bias_c"] = np.asarray(ip["L0_attn_dt_bias"], f).reshape(NH, 1)
    d["d_rep_c"] = np.ascontiguousarray(
        np.repeat(np.asarray(ip["L0_attn_D_param"], f), PH).reshape(KT, 128).T)
    d["norm_w_c"] = np.ascontiguousarray(
        np.asarray(ip["L0_attn_norm_w"], f).reshape(KT, 128).T)
    d["epsc"] = np.ascontiguousarray(np.broadcast_to(np.array([LN_EPS, RMS_EPS], f), (128, 2)))
    d["id_f"] = np.eye(128, dtype=f)
    d["id_b"] = np.eye(128, dtype=ml_dtypes.bfloat16)
    t2 = np.arange(128).reshape(128, 1)
    t1 = np.arange(128).reshape(1, 128)
    d["negmask"] = np.where(t1 >= t2, 0.0, NEGBIG).astype(f)
    d["negmask_t"] = np.ascontiguousarray(d["negmask"].T)
    rs = np.zeros((NS, NH * NS), ml_dtypes.bfloat16)
    for h in range(NH):
        rs[h, h * NS:(h + 1) * NS] = 1.0
    d["rowsel"] = rs
    oh16 = np.zeros((NE, NE * 128), ml_dtypes.bfloat16)
    for e in range(NE):
        oh16[e, e * 128:(e + 1) * 128] = 1.0
    d["onehot16"] = oh16
    return d


def build_nc(n_chunks=N_CHUNKS_DEFAULT):
    nc = bass.Bass("TRN2", target_bir_lowering=False)
    dt = nc.dram_tensor

    x = dt("x", [L, D], F32, kind="ExternalInput").ap()
    out = dt("out", [L, D], F32, kind="ExternalOutput").ap()
    w_in = dt("w_in", [MT_PROJ, 128, D], F32, kind="ExternalInput").ap()
    w_out = dt("w_out", [KT, 128, D], F32, kind="ExternalInput").ap()
    w1 = dt("w1", [NE, IT_E, 128, D], F32, kind="ExternalInput").ap()
    w3 = dt("w3", [NE, IT_E, 128, D], F32, kind="ExternalInput").ap()
    w2 = dt("w2", [NE, KT, 128, IE], F32, kind="ExternalInput").ap()
    ws1 = dt("ws1", [MT_SH, 128, D], F32, kind="ExternalInput").ap()
    ws3 = dt("ws3", [MT_SH, 128, D], F32, kind="ExternalInput").ap()
    ws2 = dt("ws2", [KT, 128, SHD], F32, kind="ExternalInput").ap()
    gwt = dt("gwt", [128, KT * NE], F32, kind="ExternalInput").ap()
    b_in_c = dt("b_in_c", [128, MT_PROJ], F32, kind="ExternalInput").ap()
    conv_w_c = dt("conv_w_c", [128, 17 * KC], F32, kind="ExternalInput").ap()
    conv_b_c = dt("conv_b_c", [128, 17], F32, kind="ExternalInput").ap()
    a_log_c = dt("a_log_c", [NS, 1], F32, kind="ExternalInput").ap()
    dt_bias_c = dt("dt_bias_c", [NH, 1], F32, kind="ExternalInput").ap()
    d_rep_c = dt("d_rep_c", [128, KT], F32, kind="ExternalInput").ap()
    norm_w_c = dt("norm_w_c", [128, KT], F32, kind="ExternalInput").ap()
    epsc_d = dt("epsc", [128, 2], F32, kind="ExternalInput").ap()
    id_f_d = dt("id_f", [128, 128], F32, kind="ExternalInput").ap()
    id_b_d = dt("id_b", [128, 128], BF16, kind="ExternalInput").ap()
    negmask_d = dt("negmask", [128, 128], F32, kind="ExternalInput").ap()
    negmask_t_d = dt("negmask_t", [128, 128], F32, kind="ExternalInput").ap()
    onehot16_d = dt("onehot16", [NE, NE * 128], BF16, kind="ExternalInput").ap()
    rowsel_d = dt("rowsel", [NS, NH * NS], BF16, kind="ExternalInput").ap()

    s_w_in = dt("s_w_in", [MT_PROJ, 128, D], BF16).ap()
    s_w_out = dt("s_w_out", [KT, 128, D], BF16).ap()
    s_w1 = dt("s_w1", [NE, IT_E, 128, D], BF16).ap()
    s_w3 = dt("s_w3", [NE, IT_E, 128, D], BF16).ap()
    s_w2 = dt("s_w2", [NE, KT, 128, IE], BF16).ap()
    s_ws1 = dt("s_ws1", [MT_SH, 128, D], BF16).ap()
    s_ws3 = dt("s_ws3", [MT_SH, 128, D], BF16).ap()
    s_ws2 = dt("s_ws2", [KT, 128, SHD], BF16).ap()

    with ExitStack() as st:
        def sb(name, shape, dtype):
            return st.enter_context(nc.sbuf_tensor(name, shape, dtype))

        def psb(name, shape, dtype):
            return st.enter_context(nc.psum_tensor(name, shape, dtype))

        res = sb("res", [128, KT * CHUNK], F32)
        xstg = sb("xstg", [128, 2 * D], F32)
        xn = sb("xn", [128, KT * CHUNK], BF16)
        ubuf = sb("ubuf", [128, 24576], BF16)
        SG_O, XS_O, YN_O = 0, KT * CHUNK, 2 * KT * CHUNK
        G_O = [0, IT_E * CHUNK]
        SH_O = 2 * IT_E * CHUNK
        stg = sb("stg", [128, 3 * CHUNK], F32)
        sqb = sb("sqb", [128, 2 * CHUNK], BF16)
        convext = sb("convext", [128, 2 * 520], F32)
        wp16 = sb("wp16", [128, 4 * D], BF16)
        wpw2 = sb("wpw2", [128, 4 * IE], BF16)
        halo = sb("halo", [128, 17 * 3], F32)
        dtb = sb("dtb", [NH, CHUNK], F32)
        ub = sb("ub", [NH, CHUNK], F32)
        bc_buf = sb("bc_buf", [64, CHUNK], BF16)
        c0b = sb("c0b", [NS, CHUNK], BF16)
        state = sb("state", [NS, NH * PH], F32)
        state_bf = sb("state_bf", [NS, NH * PH], BF16)
        XTb = sb("XTb", [128, KT * 128], BF16)
        ystage = sb("ystage", [128, KT * 128], F32)
        ysgbuf = sb("ysgbuf", [128, KT * 128], BF16)
        mtb = sb("mtb", [128, 2 * 128], BF16)
        ceb = sb("ceb", [NS, 2 * 128], BF16)
        bwb = sb("bwb", [128, 2 * NS], BF16)
        btb = sb("btb", [128, NS], BF16)
        s_u = sb("s_u", [NH, Q], F32)
        sTt = sb("sTt", [128, NS], F32)
        nsT = sb("nsT", [128, NS], F32)
        esh = sb("esh", [NH, Q], BF16)
        wdb = sb("wdb", [NH, Q], F32)
        wdb2 = sb("wdb2", [NH, Q], F32)
        WdT = sb("WdT", [128, NS], F32)
        dtT = sb("dtT", [128, NS], F32)
        etotc = sb("etotc", [NH, 1], F32)
        etr = sb("etr", [1, NH], F32)
        etba = sb("etba", [NS, NH], F32)
        scob = sb("scob", [128, 4 * NE], F32)
        cmbw = sb("cmbw", [128, 96], F32)
        combT = sb("combT", [NE, CHUNK], BF16)
        rowbuf = sb("rowbuf", [1, 2 * CHUNK + 128], F32)
        rowb_b = sb("rowb_b", [1, 2 * CHUNK + 128], BF16)
        epsc = sb("epsc_s", [128, 2], F32)
        id_f = sb("id_f_s", [128, 128], F32)
        id_b = sb("id_b_s", [128, 128], BF16)
        negmask_t = sb("negmask_t_s", [128, 128], F32)
        onehot16 = sb("onehot16_s", [NE, NE * 128], BF16)
        rowsel = sb("rowsel_s", [NS, NH * NS], BF16)
        ones_cb = sb("ones_cb", [128, 1], BF16)
        ones_rb = sb("ones_rb", [1, 128], BF16)
        ones_rf = sb("ones_rf", [1, NS], F32)
        gw_b = sb("gw_b", [128, KT * NE], BF16)
        b_in_s = sb("b_in_s", [128, MT_PROJ], F32)
        conv_w_s = sb("conv_w_s", [128, 17 * KC], F32)
        conv_b_s = sb("conv_b_s", [128, 17], F32)
        a_log_s = sb("a_log_s", [NS, 1], F32)
        negA = sb("negA", [NS, 1], F32)
        dt_bias_s = sb("dt_bias_s", [NH, 1], F32)
        d_rep_s = sb("d_rep_s", [128, KT], F32)
        norm_w_s = sb("norm_w_s", [128, KT], F32)

        pbF = [psb(f"pbF{i}", [128, 512], F32) for i in range(6)]
        pbB = [psb(f"pbB{i}", [128, 1024], BF16) for i in range(2)]

        P = Prog(nc, st)
        E = P.emit

        def K_res(i):
            return ("res", i)

        def K_xn(i):
            return ("xn", i)

        def K_pb(i):
            return ("pb", i)

        def K_pbB(i):
            return ("pB", i)

        xn_t = [xn[:, i * CHUNK:(i + 1) * CHUNK] for i in range(KT)]
        res_t = [res[:, i * CHUNK:(i + 1) * CHUNK] for i in range(KT)]
        sg_t = [ubuf[:, SG_O + i * CHUNK:SG_O + (i + 1) * CHUNK] for i in range(KT)]
        xs_t = [ubuf[:, XS_O + i * CHUNK:XS_O + (i + 1) * CHUNK] for i in range(KT)]
        yn_t = [ubuf[:, YN_O + i * CHUNK:YN_O + (i + 1) * CHUNK] for i in range(KT)]
        stg_t = [stg[:, i * CHUNK:(i + 1) * CHUNK] for i in range(3)]

        def warm(n=1, bank=4, col=0, ncol=CHUNK):
            for _ in range(n):
                E("PE", lambda e, bank=bank, col=col, ncol=ncol: e.matmul(
                    pbF[bank][:, col:col + ncol], onehot16[:, 0:128],
                    onehot16[:, 0:ncol], start=True, stop=True),
                  reads=[("cst", "onehot16")], writes=[K_pb(bank)])

        ATTN_PFX = ("sg", "xs", "yn")
        FFN_PFX = ("g", "sh")

        for name, dst, src in [
            ("epsc", epsc, epsc_d), ("id_f", id_f, id_f_d), ("id_b", id_b, id_b_d),
            ("negmaskT", negmask_t, negmask_t_d),
            ("onehot16", onehot16, onehot16_d), ("rowsel", rowsel, rowsel_d),
            ("b_in", b_in_s, b_in_c),
            ("conv_w", conv_w_s, conv_w_c), ("conv_b", conv_b_s, conv_b_c),
            ("a_log", a_log_s, a_log_c), ("dt_bias", dt_bias_s, dt_bias_c),
            ("d_rep", d_rep_s, d_rep_c), ("norm_w", norm_w_s, norm_w_c),
        ]:
            E("SY", lambda e, d_=dst, s_=src: e.dma_start(d_[:, :], s_[:, :]),
              reads=[("cstch",)], writes=[("cst", name), ("cstch",)],
              dma="cst")
        E("DVE", lambda e: e.memset(ones_cb[:], 1.0), writes=[("cst", "ones_cb")])
        E("DVE", lambda e: e.memset(ones_rb[:], 1.0), writes=[("cst", "ones_rb")])
        E("DVE", lambda e: e.memset(ones_rf[:], 1.0), writes=[("cst", "ones_rf")])
        E("DVE", lambda e: e.memset(halo[:], 0.0),
          writes=[("halo", ct) for ct in range(17)])
        E("DVE", lambda e: e.memset(state[:], 0.0),
          writes=[("st", h) for h in range(NH)])
        E("DVE", lambda e: e.memset(state_bf[:], 0.0), writes=[("stb",)])
        E("SY", lambda e: e.dma_start(stg[:, 0:KT * NE], gwt[:, :]),
          reads=[("cstch",)], writes=[("stg", 0), ("cstch",)], dma="cst")
        E("DVE", lambda e: e.tensor_copy(gw_b[:], stg[:, 0:KT * NE]),
          reads=[("stg", 0)], writes=[("cst", "gw_b")])
        E("ACT", lambda e: e.activation(negA[:], a_log_s[:], AF.Exp),
          reads=[("cst", "a_log")], writes=[("negA0",)])
        E("DVE", lambda e: e.tensor_scalar_mul(negA[:], negA[:], -1.0),
          reads=[("negA0",)], writes=[("cst", "negA")])


        scr_key = {id(s_w_in): "w_in", id(s_w_out): "w_out", id(s_ws1): "ws1",
                   id(s_ws3): "ws3", id(s_ws2): "ws2"}
        wslots = [wp16[:, i * D:(i + 1) * D] for i in range(4)]
        wskeys = [("wp16", i) for i in range(4)]
        w2slots = [wpw2[:, i * IE:(i + 1) * IE] for i in range(4)]
        w2keys = [("wpw2", i) for i in range(4)]

        c0ctr = [0]
        c0wb = [0]

        def c0_load(slot, skey, src_ap, ncols):
            i = c0ctr[0]
            c0ctr[0] += 1
            half = i % 2
            st_sl = xstg[:, half * D:half * D + ncols]
            E("SY" if half == 0 else "SC",
              lambda e, st_sl=st_sl, src_ap=src_ap: e.dma_start(st_sl, src_ap),
              writes=[("xstg", half)], dma=("c0a", "c0b")[half])
            if half == 0:
                E("DVE", lambda e, slot=slot, st_sl=st_sl, ncols=ncols:
                  e.tensor_copy(slot[:, 0:ncols], st_sl),
                  reads=[("xstg", half)], writes=[skey])
            else:
                E("ACT", lambda e, slot=slot, st_sl=st_sl, ncols=ncols:
                  e.copy(slot[:, 0:ncols], st_sl),
                  reads=[("xstg", half)], writes=[skey])

        def c0_wb(scr_slice, slot, skey, scrkey):
            i = c0wb[0]
            c0wb[0] += 1
            E("SY" if i % 2 == 0 else "SC",
              lambda e, d_=scr_slice, s_=slot: e.dma_start(d_, s_),
              reads=[skey], writes=[scrkey],
              dma=("sw0", "sw1", "sw2", "sw3")[i % 4] if i % 2 == 0
              else ("sv0", "sv1", "sv2", "sv3")[i % 4])

        def mm_stream(scr_ap, Mt, Ktn, rhs_tiles, rhs_keys, evac,
                      orig_ap=None, first=False, ldeng="SY",
                      slots=(0, 1), dmas=("wp0", "wp1"), c0gp=False):
            sname = scr_key[id(scr_ap)]
            ns = len(slots)
            for m in range(Mt):
                si = slots[m % ns]
                slot, skey = wslots[si], wskeys[si]
                if first:
                    if c0gp:
                        E("GP", lambda e, s_=slot, mm=m, a=orig_ap: e.dma_start(
                            s_, a[mm, :, :]),
                          writes=[skey], dma=f"g{m % 2}")
                    else:
                        c0_load(slot, skey, orig_ap[m, :, :], D)
                else:
                    E(ldeng, lambda e, s_=slot, mm=m, a=scr_ap: e.dma_start(
                        s_, a[mm, :, :]),
                      reads=[("scr", sname, m)], writes=[skey],
                      dma=dmas[m % ns])
                bank = pbF[m % 2]
                bkey = K_pb(m % 2)

                def mmfn(e, s_=slot, bank=bank, Ktn=Ktn):
                    ii = None
                    for k in range(Ktn):
                        ii = e.matmul(bank[:, 0:CHUNK],
                                      s_[:, k * 128:(k + 1) * 128], rhs_tiles[k],
                                      start=(k == 0), stop=(k == Ktn - 1))
                    return ii
                E("PE", mmfn, reads=[skey] + list(rhs_keys), writes=[bkey])
                if first:
                    c0_wb(scr_ap[m, :, :], slot, skey, ("scr", sname, m))
                evac(m, bank, bkey)

        def layernorm(src_keys, epscol):
            for i in range(KT):
                E("ACT", lambda e, i=i: e.copy(xn_t[i], res_t[i]),
                  reads=[src_keys[i]], writes=[K_xn(i)])
                E("DVE", lambda e, i=i: e.tensor_mul(
                    sqb[:, (i % 2) * CHUNK:(i % 2 + 1) * CHUNK],
                    xn_t[i], xn_t[i]),
                  reads=[K_xn(i)], writes=[("sqb", i % 2)])
                E("PE", lambda e, i=i: e.matmul(
                    pbF[2][0:1, 0:CHUNK], ones_cb[:], xn_t[i],
                    start=(i == 0), stop=(i == KT - 1)),
                  reads=[K_xn(i), ("cst", "ones_cb")], writes=[K_pb(2)])
                E("PE", lambda e, i=i: e.matmul(
                    pbF[3][0:1, 0:CHUNK], ones_cb[:],
                    sqb[:, (i % 2) * CHUNK:(i % 2 + 1) * CHUNK],
                    start=(i == 0), stop=(i == KT - 1)),
                  reads=[("sqb", i % 2), ("cst", "ones_cb")], writes=[K_pb(3)])
            m2 = rowbuf[0:1, CHUNK:2 * CHUNK]
            vrow = m2
            E("ACT", lambda e: e.activation(m2, pbF[2][0:1, 0:CHUNK], AF.Square,
                                            scale=1.0 / D),
              reads=[K_pb(2)], writes=[("vrow",)])
            E("ACT", lambda e: e.activation(rowb_b[0:1, 0:CHUNK],
                                            pbF[2][0:1, 0:CHUNK], AF.Copy,
                                            scale=1.0 / D),
              reads=[K_pb(2)], writes=[("mb_b",)])
            warm(2)
            E("DVE", lambda e: e.scalar_tensor_tensor(
                vrow, pbF[3][0:1, 0:CHUNK], 1.0 / D, m2, ALU.mult, ALU.subtract),
              reads=[K_pb(3), ("vrow",)], writes=[("vrow",)])
            E("ACT", lambda e: e.activation(vrow, vrow, AF.Ln,
                                            bias=epsc[0:1, epscol:epscol + 1]),
              reads=[("vrow",), ("cst", "epsc")], writes=[("vrow",)])
            warm(2)
            E("ACT", lambda e: e.activation(vrow, vrow, AF.Exp, scale=-0.5),
              reads=[("vrow",)], writes=[("vrow",)])
            E("DVE", lambda e: e.tensor_copy(rowb_b[0:1, CHUNK:2 * CHUNK], vrow),
              reads=[("vrow",)], writes=[("rr_b",)])
            warm(2)
            E("PE", lambda e: e.matmul(pbF[4][:, 0:CHUNK], ones_rb[:],
                                       rowb_b[0:1, 0:CHUNK],
                                       start=True, stop=True),
              reads=[("mb_b",), ("cst", "ones_rb")], writes=[K_pb(4)])
            E("PE", lambda e: e.matmul(pbF[5][:, 0:CHUNK], ones_rb[:],
                                       rowb_b[0:1, CHUNK:2 * CHUNK],
                                       start=True, stop=True),
              reads=[("rr_b",), ("cst", "ones_rb")], writes=[K_pb(5)])
            for i in range(KT):
                s_ = stg_t[i % 2]
                E("DVE", lambda e, i=i, s_=s_: e.tensor_sub(
                    s_, res_t[i], pbF[4][:, 0:CHUNK]),
                  reads=[src_keys[i], K_pb(4)], writes=[("stg", i % 2)])
                E("DVE", lambda e, i=i, s_=s_: e.tensor_mul(
                    xn_t[i], s_, pbF[5][:, 0:CHUNK]),
                  reads=[("stg", i % 2), K_pb(5)], writes=[K_xn(i)])

        for c in range(n_chunks):
            t0 = c * CHUNK

            P.phase = f"xload"
            for tt in range(NQ):
                sl = xstg[:, (tt % 2) * D:(tt % 2 + 1) * D]
                E("SC", lambda e, tt=tt, sl=sl: e.dma_start(
                    sl, x[t0 + tt * 128:t0 + (tt + 1) * 128, :]),
                  writes=[("xstg", tt % 2)], dma=f"xld{tt % 2}")
                for dj in range(KT):
                    bank = pbF[dj % 2]
                    E("PE", lambda e, sl=sl, dj=dj, bank=bank: e.transpose(
                        bank[:, (dj % 4) * 128:(dj % 4 + 1) * 128],
                        sl[:, dj * 128:(dj + 1) * 128], id_f[:]),
                      reads=[("xstg", tt % 2), ("cst", "id_f")],
                      writes=[K_pb(dj % 2)])
                    E("DVE", lambda e, dj=dj, tt=tt, bank=bank: e.tensor_copy(
                        res_t[dj][:, tt * 128:(tt + 1) * 128],
                        bank[:, (dj % 4) * 128:(dj % 4 + 1) * 128]),
                      reads=[K_pb(dj % 2)], writes=[K_res(dj)])

            P.phase = "LN1"
            layernorm([K_res(i) for i in range(KT)], 0)

            def proj_evac(m, bank, bkey):
                if m < KT:
                    E("ACT", lambda e, m=m, bank=bank: e.activation(
                        sg_t[m], bank[:, 0:CHUNK], AF.Silu,
                        bias=b_in_s[:, m:m + 1]),
                      reads=[bkey, ("cst", "b_in")], writes=[("sg", m)],
                      conflicts=FFN_PFX if m == 0 else ())
                    return
                ct = m - KT
                rows = 128 if ct < 16 else 64
                ce = convext[:, (ct % 2) * 520:(ct % 2) * 520 + 520]
                E("DVE", lambda e, ce=ce, ct=ct, rows=rows: e.tensor_copy(
                    ce[0:rows, 0:3], halo[0:rows, ct * 3:ct * 3 + 3]),
                  reads=[("halo", ct)], writes=[("cej", ct % 2)])
                E("ACT", lambda e, ce=ce, bank=bank, rows=rows, m=m:
                  e.activation(ce[0:rows, 3:515], bank[0:rows, 0:CHUNK],
                               AF.Identity, bias=b_in_s[0:rows, m:m + 1]),
                  reads=[bkey, ("cst", "b_in")], writes=[("ce2", ct % 2)])
                E("DVE", lambda e, ce=ce, ct=ct, rows=rows: e.tensor_copy(
                    halo[0:rows, ct * 3:ct * 3 + 3], ce[0:rows, 512:515]),
                  reads=[("ce2", ct % 2)], writes=[("halo", ct)])
                acc = stg_t[ct % 2]
                E("DVE", lambda e, ce=ce, ct=ct, rows=rows, acc=acc:
                  e.tensor_scalar_mul(acc[0:rows, :], ce[0:rows, 0:512],
                                      conv_w_s[0:rows, ct * 4:ct * 4 + 1]),
                  reads=[("cej", ct % 2), ("ce2", ct % 2), ("cst", "conv_w")],
                  writes=[("stg", ct % 2)])
                for k in range(1, KC):
                    E("DVE", lambda e, ce=ce, ct=ct, k=k, rows=rows, acc=acc:
                      e.scalar_tensor_tensor(
                          acc[0:rows, :], ce[0:rows, k:k + 512],
                          conv_w_s[0:rows, ct * 4 + k:ct * 4 + k + 1],
                          acc[0:rows, :], ALU.mult, ALU.add),
                      reads=[("cej", ct % 2), ("ce2", ct % 2), ("stg", ct % 2),
                             ("cst", "conv_w")],
                      writes=[("stg", ct % 2)])
                if ct < 16:
                    E("ACT", lambda e, ct=ct, acc=acc: e.activation(
                        xs_t[ct], acc, AF.Silu, bias=conv_b_s[:, ct:ct + 1]),
                      reads=[("stg", ct % 2), ("cst", "conv_b")],
                      writes=[("xs", ct)])
                else:
                    E("ACT", lambda e, acc=acc, ct=ct: e.activation(
                        bc_buf[:, :], acc[0:64, :], AF.Silu,
                        bias=conv_b_s[0:64, ct:ct + 1]),
                      reads=[("stg", ct % 2), ("cst", "conv_b")],
                      writes=[("bc",)])
                    E("DVE", lambda e, bank=bank, m=m: e.tensor_scalar_add(
                        stg_t[1][64:96, 0:CHUNK], bank[64:96, 0:CHUNK],
                        b_in_s[64:96, m:m + 1]),
                      reads=[bkey, ("cst", "b_in")], writes=[("stg", 1)])
                    E("SY", lambda e: e.dma_start(dtb[:, :],
                                                  stg_t[1][64:96, 0:CHUNK]),
                      reads=[("stg", 1)], writes=[("dtraw",)], dma="dtb")
                    E("SY", lambda e: e.dma_start(c0b[:, :],
                                                  bc_buf[32:64, :]),
                      reads=[("bc",)], writes=[("c0",)], dma="c0")
                    E("ACT", lambda e: e.activation(
                        ub[:, :], dtb[:, :], AF.Exp, bias=dt_bias_s[:, :]),
                      reads=[("dtraw",), ("cst", "dt_bias")], writes=[("ue",)])
                    E("ACT", lambda e: e.activation(dtb[:, :], ub[:, :], AF.Ln,
                                                    bias=1.0),
                      reads=[("ue",)], writes=[("dt",)])
                    E("DVE", lambda e: e.tensor_scalar_mul(ub[:, :], dtb[:, :],
                                                           negA[:, :]),
                      reads=[("dt",), ("cst", "negA")], writes=[("u",)])

            P.phase = "proj"
            mm_stream(s_w_in, MT_PROJ, KT, xn_t,
                      [K_xn(i) for i in range(KT)], proj_evac,
                      orig_ap=w_in, first=(c == 0), ldeng="SY",
                      slots=(0, 1), dmas=("wp0", "wp1"), c0gp=True)

            for q in range(NQ):
                P.phase = f"scan{q}"
                qs = q * 128
                dtq = dtb[:, qs:qs + 128]
                Bq = bc_buf[0:32, qs:qs + 128]
                Cq = c0b[:, qs:qs + 128]
                E("PE", lambda e, Bq=Bq: e.transpose(pbB[1][:, 896:896 + NS],
                                                     Bq, id_b[0:NS, 0:NS]),
                  reads=[("bc",), ("cst", "id_b")], writes=[K_pbB(1)])
                E("DVE", lambda e: e.tensor_copy(btb[:, :],
                                                 pbB[1][:, 896:896 + NS]),
                  reads=[K_pbB(1)], writes=[("BT",)])
                E("PE", lambda e, Bq=Bq, Cq=Cq: e.matmul(
                    pbF[3][:, 0:128], Bq, Cq, start=True, stop=True),
                  reads=[("bc",), ("c0",)], writes=[K_pb(3)])
                for ht in range(KT):
                    blk = ht % 8
                    E("PE", lambda e, ht=ht, blk=blk, qs=qs: e.transpose(
                        pbB[0][:, blk * 128:(blk + 1) * 128],
                        xs_t[ht][:, qs:qs + 128], id_b[:]),
                      reads=[("xs", ht), ("cst", "id_b")], writes=[K_pbB(0)])
                    E("DVE", lambda e, ht=ht, blk=blk: e.tensor_copy(
                        XTb[:, ht * 128:(ht + 1) * 128],
                        pbB[0][:, blk * 128:(blk + 1) * 128]),
                      reads=[K_pbB(0)], writes=[("XT", ht)])

                warm(2, bank=2)
                E("DVE", lambda e, qs=qs: e.tensor_tensor_scan(
                    s_u[:, :], ub[:, qs:qs + 128], ub[:, qs:qs + 128], 0.0,
                    ALU.add, ALU.bypass),
                  reads=[("u",)], writes=[("s_u",)])
                E("PE", lambda e: e.transpose(pbF[0][:, 0:NS], s_u[:, :],
                                              id_f[0:NS, 0:NS]),
                  reads=[("s_u",), ("cst", "id_f")], writes=[K_pb(0)])
                E("DVE", lambda e: e.tensor_copy(sTt[:, :], pbF[0][:, 0:NS]),
                  reads=[K_pb(0)], writes=[("sT",)])
                E("DVE", lambda e: e.tensor_scalar_mul(nsT[:, :], sTt[:, :],
                                                       -1.0),
                  reads=[("sT",)], writes=[("nsT",)])
                E("ACT", lambda e: e.activation(esh[:, :], s_u[:, :], AF.Exp),
                  reads=[("s_u",)], writes=[("esh",)])
                E("DVE", lambda e: e.tensor_scalar_sub(
                    wdb[:, :], s_u[:, :], s_u[:, Q - 1:Q]),
                  reads=[("s_u",)], writes=[("wdb",)])
                E("ACT", lambda e: e.activation(wdb2[:, :], wdb[:, :], AF.Exp,
                                                scale=-1.0),
                  reads=[("wdb",)], writes=[("wdb2",)])
                E("DVE", lambda e, dtq=dtq: e.tensor_mul(wdb[:, :], wdb2[:, :],
                                                         dtq),
                  reads=[("wdb2",), ("dt",)], writes=[("wd",)])
                E("PE", lambda e: e.transpose(pbF[0][:, 64:64 + NS], wdb[:, :],
                                              id_f[0:NS, 0:NS]),
                  reads=[("wd",), ("cst", "id_f")], writes=[K_pb(0)])
                E("DVE", lambda e: e.tensor_copy(WdT[:, :],
                                                 pbF[0][:, 64:64 + NS]),
                  reads=[K_pb(0)], writes=[("WdT",)])
                E("PE", lambda e, dtq=dtq: e.transpose(
                    pbF[0][:, 128:128 + NS], dtq, id_f[0:NS, 0:NS]),
                  reads=[("dt",), ("cst", "id_f")], writes=[K_pb(0)])
                E("DVE", lambda e: e.tensor_copy(dtT[:, :],
                                                 pbF[0][:, 128:128 + NS]),
                  reads=[K_pb(0)], writes=[("dtT",)])
                E("ACT", lambda e: e.activation(etotc[:, :], s_u[:, Q - 1:Q],
                                                AF.Exp),
                  reads=[("s_u",)], writes=[("etotc",)])
                E("PE", lambda e: e.transpose(pbF[0][0:1, 192:192 + NS],
                                              etotc[:, :], id_f[0:NS, 0:NS]),
                  reads=[("etotc",), ("cst", "id_f")], writes=[K_pb(0)])
                E("DVE", lambda e: e.tensor_copy(
                    etr[:, :], pbF[0][0:1, 192:192 + NS]),
                  reads=[K_pb(0)], writes=[("etr",)])
                E("PE", lambda e: e.matmul(
                    pbF[0][0:NS, 224:224 + NH], ones_rf[:, :], etr[:, :],
                    start=True, stop=True),
                  reads=[("etr",), ("cst", "ones_rf")], writes=[K_pb(0)])
                E("ACT", lambda e: e.copy(etba[:, :], pbF[0][0:NS, 224:224 + NH]),
                  reads=[K_pb(0)], writes=[("etba",)])
                def stageA(h):
                    hb = h % 2
                    h4 = h % 3
                    bank = pbF[h4]
                    colb = stg_t[2][:, hb * 128:(hb + 1) * 128]
                    E("ACT", lambda e, h=h, colb=colb: e.activation(
                        colb, negmask_t[:], AF.Identity, bias=sTt[:, h:h + 1]),
                      reads=[("sT",), ("cst", "negmaskT")],
                      writes=[("colb", hb)])
                    E("PE", lambda e, colb=colb, bank=bank: e.transpose(
                        bank[:, 0:128], colb, id_f[:]),
                      reads=[("colb", hb), ("cst", "id_f")], writes=[K_pb(h4)])
                    E("PE", lambda e, h=h, bank=bank: e.matmul(
                        bank[0:NS, 256:384], rowsel[:, h * NS:(h + 1) * NS],
                        esh[:, :], start=True, stop=True),
                      reads=[("esh",), ("cst", "rowsel")], writes=[K_pb(h4)])
                    warm(1, bank=h4, col=128, ncol=128)

                def stageB(h):
                    hb = h % 2
                    h4 = h % 3
                    bank = pbF[h4]
                    sLh = stg_t[2][:, 256 + hb * 128:256 + (hb + 1) * 128]
                    E("ACT", lambda e, h=h, sLh=sLh, bank=bank: e.activation(
                        sLh, bank[:, 0:128], AF.Exp, bias=nsT[:, h:h + 1]),
                      reads=[K_pb(h4), ("nsT",)], writes=[("sLh", hb)])
                    E("DVE", lambda e, h=h, hb=hb, sLh=sLh:
                      e.scalar_tensor_tensor(
                          mtb[:, hb * 128:(hb + 1) * 128], sLh,
                          dtT[:, h:h + 1], pbF[3][:, 0:128],
                          ALU.mult, ALU.mult),
                      reads=[("sLh", hb), ("dtT",), K_pb(3)],
                      writes=[("mt", hb)])
                    E("DVE", lambda e, hb=hb, bank=bank, Cq=Cq: e.tensor_mul(
                        ceb[:, hb * 128:(hb + 1) * 128],
                        bank[0:NS, 256:384], Cq),
                      reads=[K_pb(h4), ("c0",)], writes=[("ceh", hb)])
                    E("ACT", lambda e, h=h, hb=hb: e.activation(
                        bwb[:, hb * NS:(hb + 1) * NS], btb[:, :], AF.Copy,
                        scale=WdT[:, h:h + 1]),
                      reads=[("BT",), ("WdT",)], writes=[("bw", hb)])

                def stageCD(p):
                    h0, h1 = 2 * p, 2 * p + 1
                    yb = pbF[4 + (p % 2)]
                    ybk = K_pb(4 + (p % 2))

                    def ygrp(e, yb=yb, h0=h0, h1=h1):
                        ii = e.matmul(yb[0:64, 0:128],
                                      XTb[:, h0 * 64:(h0 + 1) * 64],
                                      mtb[:, 0:128], start=True, stop=False)
                        ii = e.matmul(yb[0:64, 0:128],
                                      state_bf[:, h0 * 64:(h0 + 1) * 64],
                                      ceb[:, 0:128], start=False, stop=True)
                        ii = e.matmul(yb[64:128, 0:128],
                                      XTb[:, h1 * 64:(h1 + 1) * 64],
                                      mtb[:, 128:256], start=True, stop=False)
                        ii = e.matmul(yb[64:128, 0:128],
                                      state_bf[:, h1 * 64:(h1 + 1) * 64],
                                      ceb[:, 128:256], start=False, stop=True)
                        return ii
                    E("PE", ygrp,
                      reads=[("XT", p), ("mt", 0), ("mt", 1), ("ceh", 0),
                             ("ceh", 1), ("stb",)],
                      writes=[ybk])
                    E("PE", lambda e, yb=yb, h0=h0: e.matmul(
                        yb[0:NS, 384:448], bwb[:, 0:NS],
                        XTb[:, h0 * 64:(h0 + 1) * 64], start=True, stop=True),
                      reads=[("bw", 0), ("XT", p)], writes=[ybk])
                    E("PE", lambda e, yb=yb, h1=h1: e.matmul(
                        yb[0:NS, 448:512], bwb[:, NS:2 * NS],
                        XTb[:, h1 * 64:(h1 + 1) * 64], start=True, stop=True),
                      reads=[("bw", 1), ("XT", p)], writes=[ybk])
                    E("DVE", lambda e, p=p, yb=yb, qs=qs: e.scalar_tensor_tensor(
                        ystage[:, p * 128:(p + 1) * 128],
                        xs_t[p][:, qs:qs + 128], d_rep_s[:, p:p + 1],
                        yb[:, 0:128], ALU.mult, ALU.add),
                      reads=[("xs", p), ("cst", "d_rep"), ybk],
                      writes=[("yst", p)])
                    E("DVE", lambda e, h0=h0, yb=yb: e.scalar_tensor_tensor(
                        state[:, h0 * 64:(h0 + 1) * 64],
                        state[:, h0 * 64:(h0 + 1) * 64], etba[:, h0:h0 + 1],
                        yb[0:NS, 384:448], ALU.mult, ALU.add),
                      reads=[("st", h0), ("etba",), ybk], writes=[("st", h0)])
                    E("DVE", lambda e, h1=h1, yb=yb: e.scalar_tensor_tensor(
                        state[:, h1 * 64:(h1 + 1) * 64],
                        state[:, h1 * 64:(h1 + 1) * 64], etba[:, h1:h1 + 1],
                        yb[0:NS, 448:512], ALU.mult, ALU.add),
                      reads=[("st", h1), ("etba",), ybk], writes=[("st", h1)])

                stageA(0)
                stageA(1)
                stageA(2)
                for h in range(NH):
                    stageB(h)
                    if h + 3 < NH:
                        stageA(h + 3)
                    if h % 2 == 1:
                        stageCD(h // 2)

                E("DVE", lambda e: e.tensor_copy(state_bf[:, :], state[:, :]),
                  reads=[("st", h) for h in range(NH)], writes=[("stb",)])

                P.phase = f"mixtail{q}"
                for ht in range(KT):
                    E("DVE", lambda e, ht=ht, qs=qs: e.tensor_mul(
                        ysgbuf[:, ht * 128:(ht + 1) * 128],
                        ystage[:, ht * 128:(ht + 1) * 128],
                        sg_t[ht][:, qs:qs + 128]),
                      reads=[("yst", ht), ("sg", ht)], writes=[("ysgb", ht)])
                    E("ACT", lambda e, ht=ht: e.activation(
                        sqb[:, (ht % 4) * 256:(ht % 4) * 256 + 128],
                        ysgbuf[:, ht * 128:(ht + 1) * 128], AF.Square),
                      reads=[("ysgb", ht)], writes=[("ysq", ht % 4)])
                    E("PE", lambda e, ht=ht: e.matmul(
                        pbF[0][0:1, 256:384], ones_cb[:],
                        sqb[:, (ht % 4) * 256:(ht % 4) * 256 + 128],
                        start=(ht == 0), stop=(ht == KT - 1)),
                      reads=[("ysq", ht % 4), ("cst", "ones_cb")],
                      writes=[K_pb(0)])
                warm(3, bank=2)
                rr = rowbuf[0:1, 2 * CHUNK:2 * CHUNK + 128]
                E("ACT", lambda e, rr=rr: e.activation(
                    rr, pbF[0][0:1, 256:384], AF.Ln, bias=epsc[0:1, 1:2],
                    scale=1.0 / D),
                  reads=[K_pb(0), ("cst", "epsc")], writes=[("rmsr",)])
                E("ACT", lambda e, rr=rr: e.activation(rr, rr, AF.Exp,
                                                       scale=-0.5),
                  reads=[("rmsr",)], writes=[("rmsr",)])
                E("DVE", lambda e, rr=rr: e.tensor_copy(
                    rowb_b[0:1, 2 * CHUNK:2 * CHUNK + 128], rr),
                  reads=[("rmsr",)], writes=[("rmsb",)])
                E("PE", lambda e: e.matmul(
                    pbF[1][:, 256:384], ones_rb[:],
                    rowb_b[0:1, 2 * CHUNK:2 * CHUNK + 128],
                    start=True, stop=True),
                  reads=[("rmsb",), ("cst", "ones_rb")], writes=[K_pb(1)])
                for ht in range(KT):
                    E("DVE", lambda e, ht=ht, qs=qs: e.scalar_tensor_tensor(
                        yn_t[ht][:, qs:qs + 128],
                        ysgbuf[:, ht * 128:(ht + 1) * 128],
                        norm_w_s[:, ht:ht + 1], pbF[1][:, 256:384],
                        ALU.mult, ALU.mult),
                      reads=[("ysgb", ht), ("cst", "norm_w"), K_pb(1)],
                      writes=[("yn", ht, q)])

            yn_keys = [("yn", i, qq) for i in range(KT) for qq in range(NQ)]

            def wout_evac(m, bank, bkey):
                E("DVE", lambda e, m=m, bank=bank: e.tensor_add(
                    res_t[m], res_t[m], bank[:, 0:CHUNK]),
                  reads=[K_res(m), bkey], writes=[K_res(m)])

            P.phase = "wout"
            mm_stream(s_w_out, KT, KT, yn_t, yn_keys, wout_evac,
                      orig_ap=w_out, first=(c == 0), ldeng="GP",
                      slots=(2, 3), dmas=("g0", "g1"), c0gp=True)

            P.phase = "LN2"
            layernorm([K_res(i) for i in range(KT)], 0)

            P.phase = "route"
            for tt in range(NQ):
                def scoremm(e, tt=tt):
                    ii = None
                    for k in range(KT):
                        ii = e.matmul(pbF[5][:, tt * NE:(tt + 1) * NE],
                                      xn_t[k][:, tt * 128:(tt + 1) * 128],
                                      gw_b[:, k * NE:(k + 1) * NE],
                                      start=(k == 0), stop=(k == KT - 1))
                    return ii
                E("PE", scoremm,
                  reads=[K_xn(k) for k in range(KT)] + [("cst", "gw_b")],
                  writes=[K_pb(5)])
                warm(2, bank=0)
                sco = scob[:, tt * NE:(tt + 1) * NE]
                E("ACT", lambda e, sco=sco, tt=tt: e.activation(
                    sco, pbF[5][:, tt * NE:(tt + 1) * NE], AF.Exp, scale=-1.0),
                  reads=[K_pb(5)], writes=[("sco", tt)])
                E("DVE", lambda e, sco=sco: e.tensor_scalar_add(sco, sco, 1.0),
                  reads=[("sco", tt)], writes=[("sco", tt)])
                E("DVE", lambda e, sco=sco: e.reciprocal(sco, sco),
                  reads=[("sco", tt)], writes=[("sco", tt)])
                gm4 = cmbw[:, 0:4]
                gmk = cmbw[:, 5:9]
                tcol = cmbw[:, 9:10]
                wsum = cmbw[:, 10:11]
                rws = cmbw[:, 11:12]
                gm4b = cmbw[:, 12:16]
                msk = cmbw[:, 16:32]
                mskw = cmbw[:, 80:96]
                ecol = cmbw[:, 32:48]
                wv = cmbw[:, 48:64]
                comb = cmbw[:, 64:80]
                m1c = cmbw[:, 4:5]
                for g_ in range(4):
                    E("DVE", lambda e, g_=g_, sco=sco: e.reduce_max(
                        gm4[:, g_:g_ + 1], sco[:, 4 * g_:4 * g_ + 4], axis=AX),
                      reads=[("sco", tt)], writes=[("gm4", tt)])
                E("DVE", lambda e: e.reduce_max(m1c, gm4, axis=AX),
                  reads=[("gm4", tt)], writes=[("m1c", tt)])
                E("DVE", lambda e: e.tensor_scalar(
                    gm4b, gm4, m1c, None, ALU.is_equal),
                  reads=[("gm4", tt), ("m1c", tt)], writes=[("gm4b", tt)])
                E("DVE", lambda e: e.scalar_tensor_tensor(
                    gm4b, gm4b, -MBIG, gm4, ALU.mult, ALU.add),
                  reads=[("gm4b", tt), ("gm4", tt)], writes=[("gm4b", tt)])
                E("DVE", lambda e: e.reduce_max(tcol, gm4b, axis=AX),
                  reads=[("gm4b", tt)], writes=[("tcol", tt)])
                E("DVE", lambda e: e.tensor_scalar(
                    gmk, gm4, tcol, None, ALU.is_ge),
                  reads=[("gm4", tt), ("tcol", tt)], writes=[("gmk", tt)])
                for g_ in range(4):
                    E("DVE", lambda e, g_=g_, sco=sco: e.tensor_scalar_mul(
                        msk[:, 4 * g_:4 * g_ + 4], sco[:, 4 * g_:4 * g_ + 4],
                        gmk[:, g_:g_ + 1]),
                      reads=[("sco", tt), ("gmk", tt)], writes=[("msk", tt)])
                E("DVE", lambda e: e.tensor_copy(mskw, msk),
                  reads=[("msk", tt)], writes=[("mskw", tt)])
                for _ in range(3):
                    E("DVE", lambda e: e.reduce_max(tcol, mskw, axis=AX),
                      reads=[("mskw", tt)], writes=[("tcol", tt)])
                    E("DVE", lambda e: e.tensor_scalar(
                        ecol, mskw, tcol, None, ALU.is_equal),
                      reads=[("mskw", tt), ("tcol", tt)], writes=[("ecol", tt)])
                    E("DVE", lambda e: e.scalar_tensor_tensor(
                        mskw, ecol, -MBIG, mskw, ALU.mult, ALU.add),
                      reads=[("ecol", tt), ("mskw", tt)], writes=[("mskw", tt)])
                E("DVE", lambda e: e.reduce_max(tcol, mskw, axis=AX),
                  reads=[("mskw", tt)], writes=[("tcol", tt)])
                E("DVE", lambda e: e.tensor_scalar(
                    ecol, msk, tcol, None, ALU.is_ge),
                  reads=[("msk", tt), ("tcol", tt)], writes=[("ecol", tt)])
                E("DVE", lambda e, sco=sco: e.tensor_mul(wv, sco, ecol),
                  reads=[("ecol", tt), ("sco", tt)], writes=[("wv", tt)])
                E("DVE", lambda e: e.tensor_reduce(wsum, wv, AX, ALU.add),
                  reads=[("wv", tt)], writes=[("wsum", tt)])
                E("DVE", lambda e: e.reciprocal(rws, wsum),
                  reads=[("wsum", tt)], writes=[("rws", tt)])
                E("DVE", lambda e: e.tensor_scalar_mul(comb, wv, rws),
                  reads=[("wv", tt), ("rws", tt)], writes=[("comb", tt)])
                E("PE", lambda e, tt=tt: e.transpose(
                    pbF[4][0:NE, tt * 128:(tt + 1) * 128], comb, id_f[:]),
                  reads=[("comb", tt), ("cst", "id_f")], writes=[K_pb(4)])
                E("DVE", lambda e, tt=tt: e.tensor_copy(
                    combT[:, tt * 128:(tt + 1) * 128],
                    pbF[4][0:NE, tt * 128:(tt + 1) * 128]),
                  reads=[K_pb(4)], writes=[("combT", tt)])
                warm(2, bank=0)

            for e_i in range(NE):
                P.phase = "expert"
                gb = e_i % 2
                for it in range(IT_E):
                    s1, k1 = wslots[it % 2], wskeys[it % 2]
                    s3, k3 = wslots[2 + it % 2], wskeys[2 + it % 2]
                    if c == 0:
                        E("GP", lambda e, s1=s1, e_i=e_i, it=it: e.dma_start(
                            s1, w1[e_i, it, :, :]),
                          writes=[k1], dma=f"g{it % 2}")
                        E("GP", lambda e, s3=s3, e_i=e_i, it=it: e.dma_start(
                            s3, w3[e_i, it, :, :]),
                          writes=[k3], dma=f"g{2 + it % 2}")
                        c0_wb(s_w1[e_i, it, :, :], s1, k1,
                              ("scr", "w1", e_i, it))
                        c0_wb(s_w3[e_i, it, :, :], s3, k3,
                              ("scr", "w3", e_i, it))
                    else:
                        E("GP", lambda e, s1=s1, e_i=e_i, it=it: e.dma_start(
                            s1, s_w1[e_i, it, :, :]),
                          reads=[("scr", "w1", e_i, it)], writes=[k1],
                          dma=f"g{2 + it % 2}")
                        E("SY", lambda e, s3=s3, e_i=e_i, it=it: e.dma_start(
                            s3, s_w3[e_i, it, :, :]),
                          reads=[("scr", "w3", e_i, it)], writes=[k3],
                          dma=f"wp{2 + it % 2}")

                    def h1mm(e, s1=s1):
                        ii = None
                        for k in range(KT):
                            ii = e.matmul(pbF[0][:, 0:CHUNK],
                                          s1[:, k * 128:(k + 1) * 128], xn_t[k],
                                          start=(k == 0), stop=(k == KT - 1))
                        return ii
                    E("PE", h1mm, reads=[k1] + [K_xn(k) for k in range(KT)],
                      writes=[K_pb(0)])

                    def h3mm(e, s3=s3):
                        ii = None
                        for k in range(KT):
                            ii = e.matmul(pbF[1][:, 0:CHUNK],
                                          s3[:, k * 128:(k + 1) * 128], xn_t[k],
                                          start=(k == 0), stop=(k == KT - 1))
                        return ii
                    E("PE", h3mm, reads=[k3] + [K_xn(k) for k in range(KT)],
                      writes=[K_pb(1)])
                    if it == 0:
                        E("PE", lambda e, e_i=e_i: e.matmul(
                            pbF[2][:, 0:CHUNK],
                            onehot16[:, e_i * 128:(e_i + 1) * 128],
                            combT[:, :], start=True, stop=True),
                          reads=[("combT", tt) for tt in range(NQ)] +
                                [("cst", "onehot16")],
                          writes=[K_pb(2)])
                    E("ACT", lambda e: e.activation(stg_t[0], pbF[0][:, 0:CHUNK],
                                                    AF.Silu),
                      reads=[K_pb(0)], writes=[("stg", 0)])
                    E("DVE", lambda e: e.tensor_mul(stg_t[1], stg_t[0],
                                                    pbF[1][:, 0:CHUNK]),
                      reads=[("stg", 0), K_pb(1)], writes=[("stg", 1)])
                    E("DVE", lambda e, gb=gb, it=it: e.tensor_mul(
                        ubuf[:, G_O[gb] + it * CHUNK:G_O[gb] + (it + 1) * CHUNK],
                        stg_t[1], pbF[2][:, 0:CHUNK]),
                      reads=[("stg", 1), K_pb(2)], writes=[("g", gb, it)],
                      conflicts=ATTN_PFX if (e_i == 0 and it == 0) else ())
                P.phase = "w2"
                for m in range(KT):
                    sl, kk = w2slots[m % 4], w2keys[m % 4]
                    if c == 0:
                        E("GP", lambda e, sl=sl, e_i=e_i, m=m: e.dma_start(
                            sl, w2[e_i, m, :, :]),
                          writes=[kk], dma=f"h{m % 4}")
                        c0_wb(s_w2[e_i, m, :, :], sl, kk,
                              ("scr", "w2", e_i, m))
                    else:
                        E("SC", lambda e, sl=sl, e_i=e_i, m=m: e.dma_start(
                            sl, s_w2[e_i, m, :, :]),
                          reads=[("scr", "w2", e_i, m)], writes=[kk],
                          dma=f"w2{m % 4}")

                    def w2mm(e, sl=sl, gb=gb, m=m):
                        ii = None
                        for k in range(IT_E):
                            ii = e.matmul(
                                pbF[3 + m % 2][:, 0:CHUNK],
                                sl[:, k * 128:(k + 1) * 128],
                                ubuf[:, G_O[gb] + k * CHUNK:
                                     G_O[gb] + (k + 1) * CHUNK],
                                start=(k == 0), stop=(k == IT_E - 1))
                        return ii
                    E("PE", w2mm,
                      reads=[kk] + [("g", gb, k) for k in range(IT_E)],
                      writes=[K_pb(3 + m % 2)])
                    E("DVE", lambda e, m=m: e.tensor_add(
                        res_t[m], res_t[m], pbF[3 + m % 2][:, 0:CHUNK]),
                      reads=[K_res(m), K_pb(3 + m % 2)], writes=[K_res(m)])

            P.phase = "shared"
            for m in range(MT_SH):
                s1, k1 = wslots[m % 2], wskeys[m % 2]
                s3, k3 = wslots[2 + m % 2], wskeys[2 + m % 2]
                if c == 0:
                    E("GP", lambda e, s1=s1, m=m: e.dma_start(
                        s1, ws1[m, :, :]),
                      writes=[k1], dma=f"g{m % 2}")
                    E("GP", lambda e, s3=s3, m=m: e.dma_start(
                        s3, ws3[m, :, :]),
                      writes=[k3], dma=f"g{2 + m % 2}")
                    c0_wb(s_ws1[m, :, :], s1, k1, ("scr", "ws1", m))
                    c0_wb(s_ws3[m, :, :], s3, k3, ("scr", "ws3", m))
                else:
                    E("GP", lambda e, s1=s1, m=m: e.dma_start(
                        s1, s_ws1[m, :, :]),
                      reads=[("scr", "ws1", m)], writes=[k1], dma=f"g{2 + m % 2}")
                    E("SY", lambda e, s3=s3, m=m: e.dma_start(
                        s3, s_ws3[m, :, :]),
                      reads=[("scr", "ws3", m)], writes=[k3],
                      dma=f"wp{2 + m % 2}")

                def sh1mm(e, s1=s1):
                    ii = None
                    for k in range(KT):
                        ii = e.matmul(pbF[0][:, 0:CHUNK],
                                      s1[:, k * 128:(k + 1) * 128], xn_t[k],
                                      start=(k == 0), stop=(k == KT - 1))
                    return ii
                E("PE", sh1mm, reads=[k1] + [K_xn(k) for k in range(KT)],
                  writes=[K_pb(0)])

                def sh3mm(e, s3=s3):
                    ii = None
                    for k in range(KT):
                        ii = e.matmul(pbF[1][:, 0:CHUNK],
                                      s3[:, k * 128:(k + 1) * 128], xn_t[k],
                                      start=(k == 0), stop=(k == KT - 1))
                    return ii
                E("PE", sh3mm, reads=[k3] + [K_xn(k) for k in range(KT)],
                  writes=[K_pb(1)])
                E("ACT", lambda e: e.activation(stg_t[0], pbF[0][:, 0:CHUNK],
                                                AF.Silu),
                  reads=[K_pb(0)], writes=[("stg", 0)])
                E("DVE", lambda e, m=m: e.tensor_mul(
                    ubuf[:, SH_O + m * CHUNK:SH_O + (m + 1) * CHUNK],
                    stg_t[0], pbF[1][:, 0:CHUNK]),
                  reads=[("stg", 0), K_pb(1)], writes=[("sh", m)])
            P.phase = "ws2"
            for m in range(KT):
                sa, ka = w2slots[2 + m % 2], w2keys[2 + m % 2]
                sb_, kb = w2slots[m % 2], w2keys[m % 2]
                if c == 0:
                    E("GP", lambda e, sa=sa, m=m: e.dma_start(
                        sa, ws2[m, :, 0:IT_E * 128]),
                      writes=[ka], dma=f"h{2 + m % 2}")
                    E("GP", lambda e, sb_=sb_, m=m: e.dma_start(
                        sb_, ws2[m, :, IT_E * 128:SHD]),
                      writes=[kb], dma=f"h{m % 2}")
                    c0_wb(s_ws2[m, :, 0:IT_E * 128], sa, ka,
                          ("scr", "ws2", m, 0))
                    c0_wb(s_ws2[m, :, IT_E * 128:SHD], sb_, kb,
                          ("scr", "ws2", m, 1))
                else:
                    E("SC", lambda e, sa=sa, m=m: e.dma_start(
                        sa, s_ws2[m, :, 0:IT_E * 128]),
                      reads=[("scr", "ws2", m, 0)], writes=[ka],
                      dma=f"w2{2 + m % 2}")
                    E("SC", lambda e, sb_=sb_, m=m: e.dma_start(
                        sb_, s_ws2[m, :, IT_E * 128:SHD]),
                      reads=[("scr", "ws2", m, 1)], writes=[kb],
                      dma=f"w2{m % 2}")

                def ws2mm(e, sa=sa, sb_=sb_, m=m):
                    ii = None
                    for k in range(MT_SH):
                        if k < IT_E:
                            lhs = sa[:, k * 128:(k + 1) * 128]
                        else:
                            lhs = sb_[:, (k - IT_E) * 128:(k - IT_E + 1) * 128]
                        ii = e.matmul(pbF[3 + m % 2][:, 0:CHUNK], lhs,
                                      ubuf[:, SH_O + k * CHUNK:
                                           SH_O + (k + 1) * CHUNK],
                                      start=(k == 0), stop=(k == MT_SH - 1))
                    return ii
                E("PE", ws2mm,
                  reads=[ka, kb] + [("sh", k) for k in range(MT_SH)],
                  writes=[K_pb(3 + m % 2)])
                E("DVE", lambda e, m=m: e.tensor_add(
                    res_t[m], res_t[m], pbF[3 + m % 2][:, 0:CHUNK]),
                  reads=[K_res(m), K_pb(3 + m % 2)], writes=[K_res(m)])

            P.phase = "ostore"
            for tt in range(NQ):
                sl = xstg[:, (tt % 2) * D:(tt % 2 + 1) * D]
                for dj in range(KT):
                    bank = pbF[dj % 2]
                    E("PE", lambda e, dj=dj, tt=tt, bank=bank: e.transpose(
                        bank[:, (dj % 4) * 128:(dj % 4 + 1) * 128],
                        res_t[dj][:, tt * 128:(tt + 1) * 128], id_f[:]),
                      reads=[K_res(dj), ("cst", "id_f")], writes=[K_pb(dj % 2)])
                    E("DVE", lambda e, dj=dj, sl=sl, bank=bank: e.tensor_copy(
                        sl[:, dj * 128:(dj + 1) * 128],
                        bank[:, (dj % 4) * 128:(dj % 4 + 1) * 128]),
                      reads=[K_pb(dj % 2)],
                      writes=[("xstg", tt % 2, dj), ("xstg", tt % 2)])
                E("SC", lambda e, sl=sl, tt=tt: e.dma_start(
                    out[t0 + tt * 128:t0 + (tt + 1) * 128, :], sl),
                  reads=[("xstg", tt % 2, dj) for dj in range(KT)],
                  writes=[("xstg", tt % 2)], dma=f"ost{tt % 2}")

        P.barrier_all()
    build_nc.last_prog = P
    return nc


def kernel(**inputs):
    x = np.asarray(inputs["L0_attn_x"], np.float32)
    d = _prep_inputs(inputs)
    nc = build_nc()
    in_maps = []
    for b in range(N_CORES):
        m = dict(d)
        m["x"] = np.ascontiguousarray(x[b])
        in_maps.append(m)
    res = run_bass_kernel_spmd(nc, in_maps, core_ids=list(range(N_CORES)))
    return np.stack([np.asarray(r["out"], np.float32) for r in res.results])
